# version 03
# baseline (speedup 1.0000x reference)
import contextlib
import math
import numpy as np
import concourse.bass as bass
import concourse.mybir as mybir
from concourse.bass import IndirectOffsetOnAxis
from concourse.bass_utils import run_bass_kernel_spmd

F32 = mybir.dt.float32
BF16 = mybir.dt.bfloat16
I32 = mybir.dt.int32
AF = mybir.ActivationFunctionType
ALU = mybir.AluOpType
AX = mybir.AxisListType

T = 4096
D = 1024
NT = T // 128
ND = D // 128

ENGS = ("sync", "scalar", "vector", "tensor", "gpsimd")


class _Step:
    __slots__ = ("eng", "fn", "idx", "waits", "inc_cell")

    def __init__(self, eng, fn, idx):
        self.eng = eng
        self.fn = fn
        self.idx = idx
        self.waits = []
        self.inc_cell = None


class _Ev:
    __slots__ = ("kind", "eng", "step", "sem", "val")

    def __init__(self, kind, eng=None, step=None, sem=None, val=None):
        self.kind = kind
        self.eng = eng
        self.step = step
        self.sem = sem
        self.val = val


class Res:
    __slots__ = ("name", "writer", "readers")

    def __init__(self, name):
        self.name = name
        self.writer = None
        self.readers = []


class Stream:

    def __init__(self, ctx, name):
        self.ctx = ctx
        self.name = name
        self.sem = ctx.new_sem("dma_" + name)
        self.count = 0


class Ctx:
    def __init__(self, nc, stack):
        self.nc = nc
        self.stack = stack
        self.steps = {e: [] for e in ENGS}
        self.eng_sem = {e: self.new_sem("eng_" + e) for e in ENGS}
        self.known = {c: {p: -1 for p in ENGS} for c in ENGS}
        self.known_dma = {c: {} for c in ENGS}
        self.milestones = {e: [] for e in ENGS}
        self._sbuf_bytes = 0
        self._res = []
        self._bufstreams = {}

    def new_sem(self, name):
        return self.stack.enter_context(self.nc.semaphore(name))

    def sb(self, name, shape, dtype):
        t = self.stack.enter_context(self.nc.sbuf_tensor(name, list(shape), dtype))
        self._sbuf_bytes += shape[1] * mybir.dt.size(dtype)
        return t

    def ps(self, name, shape, dtype=F32):
        return self.stack.enter_context(self.nc.psum_tensor(name, list(shape), dtype))

    def res(self, name):
        r = Res(name)
        self._res.append(r)
        return r

    def bufstream(self, res, eng):
        key = (id(res), eng == "gpsimd")
        s = self._bufstreams.get(key)
        if s is None:
            s = Stream(self, "b%d_%s" % (len(self._bufstreams), res.name))
            self._bufstreams[key] = s
        return s

    def _need(self, cons_eng, ev, waits):
        if ev is None:
            return
        if ev.kind == "dma":
            kd = self.known_dma[cons_eng]
            if kd.get(ev.sem.name if hasattr(ev.sem, "name") else id(ev.sem), -1) >= ev.val:
                return
            kd[ev.sem.name if hasattr(ev.sem, "name") else id(ev.sem)] = ev.val
            waits.append((ev.sem, ev.val))
        else:
            if ev.eng == cons_eng and cons_eng in ("tensor", "sync"):
                return
            if self.known[cons_eng][ev.eng] >= ev.step.idx:
                return
            self.known[cons_eng][ev.eng] = ev.step.idx
            if ev.step.inc_cell is None:
                ev.step.inc_cell = [self.eng_sem[ev.eng], None]
                self.milestones[ev.eng].append(ev.step)
            waits.append((ev.step.inc_cell, None))

    def step(self, eng, fn, reads=(), writes=(), dma=False):
        st = _Step(eng, fn, len(self.steps[eng]))
        waits = st.waits
        for r in reads:
            self._need(eng, r.writer, waits)
        for r in writes:
            self._need(eng, r.writer, waits)
            for rd in r.readers:
                self._need(eng, rd, waits)
        self.steps[eng].append(st)
        if dma:
            stream = self.bufstream(writes[0], eng)
            stream.count += 1
            ev = _Ev("dma", sem=stream.sem, val=16 * stream.count)
            st.inc_cell = [stream.sem, None]
        else:
            ev = _Ev("step", eng=eng, step=st)
        for r in writes:
            r.writer = ev
            r.readers = []
        for r in reads:
            r.readers.append(ev)
        return ev

    def barrier_wait(self, eng, ev):
        st = _Step(eng, lambda e: None, len(self.steps[eng]))
        self._need(eng, ev, st.waits)
        self.steps[eng].append(st)

    def finalize(self, block):
        for eng in ENGS:
            ms = sorted(self.milestones[eng], key=lambda s: s.idx)
            for i, st in enumerate(ms):
                st.inc_cell[1] = i + 1
        for eng in ENGS:
            steps = self.steps[eng]
            if not steps:
                continue

            def body(e, steps=steps):
                for st in steps:
                    for sem, val in st.waits:
                        if isinstance(sem, list):
                            e.wait_ge(sem[0], sem[1])
                        else:
                            e.wait_ge(sem, val)
                    ins = st.fn(e)
                    if st.inc_cell is not None:
                        assert ins is not None, f"step on {eng} idx {st.idx} returned no instruction but needs inc"
                        amt = 16 if st.inc_cell[1] is None else 1
                        ins.then_inc(st.inc_cell[0], amt)
            getattr(block, eng)(body)


WSPECS = {
    "L0_attn_Wq": (1024, 1024), "L0_attn_Wk": (1024, 1024),
    "L0_attn_Wv": (1024, 1024), "L0_attn_Wo": (1024, 1024),
    "L0_attn_W_gamma": (2, 512), "L0_attn_b_gamma": (2, 512),
    "L0_attn_W_lambda": (2, 512), "L0_attn_b_lambda": (2, 512),
    "L0_ffn_Wr": (1024, 8), "L0_ffn_br": (8,),
    "L0_ffn_W1": (8, 1024, 4096), "L0_ffn_b1": (8, 4096),
    "L0_ffn_W2": (8, 4096, 1024), "L0_ffn_b2": (8, 1024),
    "L1_attn_wq": (1024, 16, 64), "L1_attn_wk": (1024, 16, 64),
    "L1_attn_wv": (1024, 16, 64), "L1_attn_wo": (1024, 1024),
    "L1_attn_omega": (256, 64),
    "L1_attn_ln1_g": (1024,), "L1_attn_ln1_b": (1024,),
    "L1_attn_w1": (1024, 4096), "L1_attn_b1": (4096,),
    "L1_attn_w2": (4096, 1024), "L1_attn_b2": (1024,),
    "L1_attn_ln2_g": (1024,), "L1_attn_ln2_b": (1024,),
    "L1_ffn_Wg": (8, 1024), "L1_ffn_W1": (8, 4096, 1024), "L1_ffn_b1": (8, 4096),
    "L1_ffn_Wo": (1024, 2048), "L1_ffn_bo": (1024,),
    "L2_attn_Wq": (1024, 1024), "L2_attn_Wk": (1024, 1024),
    "L2_attn_Wv": (1024, 1024), "L2_attn_Wo": (1024, 1024),
    "L2_ffn_W_fc": (4096, 1024), "L2_ffn_ln_w": (4096,), "L2_ffn_ln_b": (4096,),
    "L2_ffn_W_proj": (1024, 4096),
    "L3_attn_Wq": (1024, 1024), "L3_attn_bq": (1024,),
    "L3_attn_Wk": (1024, 1024), "L3_attn_bk": (1024,),
    "L3_attn_Wv": (1024, 1024), "L3_attn_bv": (1024,),
    "L3_attn_Wo": (1024, 1024), "L3_attn_bo": (1024,),
    "L3_ffn_wg_red": (16, 1024), "L3_ffn_wg": (8, 16),
    "L3_ffn_w1": (8, 1024, 4096), "L3_ffn_b1": (8, 4096),
    "L3_ffn_w2": (8, 4096, 1024), "L3_ffn_b2": (8, 1024),
}

SCRATCH_BF16 = [
    "L0_attn_Wq", "L0_attn_Wk", "L0_attn_Wv", "L0_attn_Wo",
    "L0_ffn_W1", "L0_ffn_W2",
    "L1_attn_omega",
    "L1_ffn_Wg", "L1_ffn_W1", "L1_ffn_Wo",
    "L2_attn_Wq", "L2_attn_Wk", "L2_attn_Wv", "L2_attn_Wo",
    "L3_attn_Wq", "L3_attn_Wk", "L3_attn_Wv", "L3_attn_Wo",
    "L3_ffn_w1", "L3_ffn_w2",
]


def ceil_div(a, b):
    return (a + b - 1) // b



class Prog:

    def __init__(self, layers):
        self.layers = tuple(layers)
        self.stack = contextlib.ExitStack()
        self.nc = bass.Bass("TRN2", target_bir_lowering=False)
        self.c = Ctx(self.nc, self.stack)
        self._declare_io()
        self._declare_scratch()
        self._declare_sbuf()

    def _declare_io(self):
        nc = self.nc
        self.x_in = nc.dram_tensor("x", [T, D], F32, kind="ExternalInput").ap()
        self.r_xin = None
        self.out_ext = nc.dram_tensor("out", [T, D], F32, kind="ExternalOutput").ap()
        self.W = {}
        for name, shp in WSPECS.items():
            if int(name[1]) in self.layers:
                self.W[name] = nc.dram_tensor(name, list(shp), F32, kind="ExternalInput").ap()

    def _declare_scratch(self):
        nc = self.nc
        c = self.c

        def dram(name, shape, dtype):
            ap = nc.dram_tensor(name, list(shape), dtype, kind="Internal").ap()
            return ap, c.res(name)

        self.r_xin = c.res("x_in")
        self.r_out = c.res("out_ext")
        self.xres = [dram("xresA", (T, D), F32), dram("xresB", (T, D), F32)]
        self.xn_d = dram("xn_d", (T, D), BF16)
        self.aout_d = dram("aout_d", (T, D), F32)
        self.fout_d = dram("fout_d", (T, D), F32)
        self.qT_d = dram("qT_d", (D, T), BF16)
        self.kT_d = dram("kT_d", (D, T), BF16)
        self.h_d = dram("h_d", (T, 4096), BF16)
        self.xnlo_d = dram("xnlo_d", (T, D), BF16)
        self.xnf_d = dram("xnf_d", (T, D), F32)
        self.x1_d = dram("x1_d", (T, D), F32)
        self.usq_d = dram("usq_d", (1, 4096), F32)
        self.pq_d = dram("pq_d", (T, 256), BF16)
        self.hT_d = dram("hT_d", (4096, T), BF16)
        self.wscr = {}
        for name in SCRATCH_BF16:
            if name in self.W:
                self.wscr[name] = dram("scr_" + name, WSPECS[name], BF16)
        if 2 in self.layers:
            self.wfcq_d = dram("wfcq_d", (4096, 1024), BF16)
            self.wprojq_d = dram("wprojq_d", (1024, 4096), BF16)
        self._dram = dram

    def _declare_sbuf(self):
        c = self.c
        self.big8 = c.sb("big8", (128, 32768), BF16)
        self.r_big8 = c.res("big8")
        self.vres = c.sb("vres", (128, 33280), BF16)
        self.r_vres = c.res("vres")
        self.pbf = [c.sb("pbf%d" % i, (128, 4096), BF16) for i in range(3)]
        self.r_pbf = [c.res("pbf%d" % i) for i in range(3)]
        self.pf = [c.sb("pf%d" % i, (128, 1024), F32) for i in range(4)]
        self.r_pf = [c.res("pf%d" % i) for i in range(4)]
        self.wst = [c.sb("wst%d" % i, (128, 2048), BF16) for i in range(2)]
        self.r_wst = [c.res("wst%d" % i) for i in range(2)]
        self.wslab8 = c.sb("wslab8", (128, 8192), BF16)
        self.r_wslab8 = c.res("wslab8")
        self.r_w8h = [c.res("w8a"), c.res("w8b")]
        self.scol = c.sb("scol", (128, 128), F32)
        self.r_scol = [c.res("scol%d" % i) for i in range(16)]
        self.rcol = c.sb("rcol", (128, 256), F32)
        self.r_rcol = c.res("rcol")
        self.weff = c.sb("weff", (128, 128), F32)
        self.weff_bf = c.sb("weff_bf", (128, 256), BF16)
        self.r_weff = c.res("weff")
        self.cpool = c.sb("cpool", (128, 32), F32)
        self.cpool_vals = {}
        for v in (1e-5, 1e-6, 1e-9, 1e-4, 0.5e-6, math.pi / 2.0, 1.0, 1e-12):
            i = len(self.cpool_vals)
            self.cpool_vals[v] = i
            self.nc.const_aps.register(F32, v, self.cpool[:, i:i + 1])
        self.row1k = c.sb("row1k", (1, 1024), F32)
        self.r_row1k = c.res("row1k")
        self.ident = c.sb("ident", (128, 128), BF16)
        self.ident_f = c.sb("ident_f", (128, 128), F32)
        self.onescol = c.sb("onescol", (128, 8), F32)
        self.onescol_bf = c.sb("onescol_bf", (128, 8), BF16)
        self.onesrow_f = c.sb("onesrow_f", (1, 128), F32)
        self.tril2_f = c.sb("tril2_f", (128, 128), F32)
        self.masku = c.sb("masku", (64, 64), BF16)
        self.row63 = c.sb("row63", (128, 128), BF16)
        self.row127 = c.sb("row127", (128, 128), BF16)
        self.prot = c.sb("prot", (128, 256), BF16)
        self.fstat = c.sb("fstat", (128, 256), F32)
        self.r_fstat = c.res("fstat")
        self.idx_all = c.sb("idx_all", (128, 64), I32)
        self.r_idx = c.res("idx_all")
        self.wcol2 = c.sb("wcol2", (128, 64), F32)
        self.r_wcol2 = c.res("wcol2")
        self.crow = c.sb("crow", (128, 24), F32)
        self.r_crow = c.res("crow")
        self.r_xetB = c.res("xetB")
        self.r_const = c.res("const")
        self.psA = c.ps("psA", (128, 2048), F32)
        self.psB = c.ps("psB", (128, 2048), F32)
        self.r_ps = [c.res("ps%d" % i) for i in range(8)]

    def psv(self, i):
        if i < 4:
            return self.psA[:, 512 * i:512 * (i + 1)]
        return self.psB[:, 512 * (i - 4):512 * (i - 3)]

    def emit_consts(self):
        c = self.c
        rc = self.r_const

        def g(fn):
            return c.step("gpsimd", fn, writes=[rc])
        g(lambda e: e.memset(self.onescol[:, :], 1.0))
        g(lambda e: e.memset(self.onesrow_f[:, :], 1.0))
        for v, i in self.cpool_vals.items():
            g(lambda e, v=v, i=i: e.memset(self.cpool[:, i:i + 1], v))
        g(lambda e: e.memset(self.onescol_bf[:, :], 1.0))
        g(lambda e: e.memset(self.ident[:, :], 1.0))
        g(lambda e: e.affine_select(out=self.ident[:, :], in_=self.ident[:, :],
                                    pattern=[[-1, 128]], compare_op=ALU.is_equal,
                                    fill=0.0, base=0, channel_multiplier=1))
        g(lambda e: e.memset(self.ident_f[:, :], 1.0))
        g(lambda e: e.affine_select(out=self.ident_f[:, :], in_=self.ident_f[:, :],
                                    pattern=[[-1, 128]], compare_op=ALU.is_equal,
                                    fill=0.0, base=0, channel_multiplier=1))

    def emit_prepass(self):
        c = self.c
        for name, scr in self.wscr.items():
            w = self.W[name]
            src, (scr_ap, scr_res) = w, scr
            shp = WSPECS[name]
            if len(shp) == 2:
                mats = [(src, scr_ap, shp)]
            else:
                mats = [(src[e], scr_ap[e], shp[1:]) for e in range(shp[0])]
            for (s_ap, d_ap, (rows, cols)) in mats:
                r0 = 0
                while r0 < rows:
                    rr = min(128, rows - r0)
                    buf = self.pbf[(r0 // 128) % 2]
                    rbuf = self.r_pbf[(r0 // 128) % 2]
                    c.step("gpsimd",
                           lambda e, s=s_ap, r0=r0, rr=rr, b=buf, cols=cols:
                           e.dma_start(out=b[0:rr, 0:cols], in_=s[r0:r0 + rr, :]),
                           writes=[rbuf], dma=True)
                    c.step("sync",
                           lambda e, d=d_ap, r0=r0, rr=rr, b=buf, cols=cols:
                           e.dma_start(out=d[r0:r0 + rr, :], in_=b[0:rr, 0:cols]),
                           reads=[rbuf], writes=[scr_res], dma=True)
                    r0 += 128

    def emit_ln_merge(self, src, adds, dst_res, dst_xn, eps=1e-5, src_res=None,
                      add_res=(), dstr_res=None, dstxn_res=None, emit_lo=False,
                      emit_f32=False, gamma_beta=None, dst_f32=None, dstf_res=None):
        c = self.c
        for i in range(NT):
            p = i % 2
            xt, rxt = self.pf[p], self.r_pf[p]
            sqt, rsq = self.pf[2 + p], self.r_pf[2 + p]
            t0 = 128 * i
            rds = [src_res] if src_res is not None else []
            c.step("gpsimd", lambda e, t0=t0, xt=xt: e.dma_start(out=xt[:, :], in_=src[t0:t0 + 128, :]),
                   reads=rds, writes=[rxt], dma=True)
            for a_ap, a_res in zip(adds, add_res):
                c.step("gpsimd", lambda e, t0=t0, xt=xt, a=a_ap:
                       e.dma_start(out=xt[:, :], in_=a[t0:t0 + 128, :], accum_op=ALU.add),
                       reads=[a_res], writes=[rxt], dma=True)
            if dst_res is not None:
                c.step("sync", lambda e, t0=t0, xt=xt: e.dma_start(out=dst_res[t0:t0 + 128, :], in_=xt[:, :]),
                       reads=[rxt], writes=[dstr_res], dma=True)
            if dst_xn is None and dst_f32 is None:
                continue
            s1 = self.scol[:, 8 * p:8 * p + 1]
            ssq = self.scol[:, 8 * p + 1:8 * p + 2]
            mu = self.scol[:, 8 * p + 2:8 * p + 3]
            rstd = self.scol[:, 8 * p + 3:8 * p + 4]
            rsc = self.r_scol[p]
            c.step("vector", lambda e, xt=xt, s1=s1: e.reduce_sum(s1, xt[:, :], axis=AX.X),
                   reads=[rxt], writes=[rsc])
            c.step("scalar", lambda e, xt=xt, sqt=sqt, ssq=ssq:
                   e.activation(out=sqt[:, :], in_=xt[:, :], func=AF.Square, accum_out=ssq),
                   reads=[rxt], writes=[rsq, rsc])
            c.step("vector", lambda e, s1=s1, mu=mu: e.tensor_scalar_mul(mu, s1, 1.0 / D),
                   reads=[rsc], writes=[rsc])
            c.step("vector", lambda e, ssq=ssq, mu=mu:
                   e.tensor_scalar(ssq, ssq, 1.0 / D, None, ALU.mult),
                   reads=[rsc], writes=[rsc])
            c.step("vector", lambda e, ssq=ssq, mu=mu:
                   e.scalar_tensor_tensor(ssq, mu, mu, ssq, ALU.mult, ALU.subtract),
                   reads=[rsc], writes=[rsc])
            c.step("scalar", lambda e, ssq=ssq: e.activation(out=ssq, in_=ssq, func=AF.Ln, scale=-1.0, bias=eps),
                   reads=[rsc, self.r_const], writes=[rsc])
            c.step("scalar", lambda e, ssq=ssq, rstd=rstd: e.activation(out=rstd, in_=ssq, func=AF.Exp, scale=-0.5),
                   reads=[rsc], writes=[rsc])
            xnt, rxn = self.pbf[p], self.r_pbf[p]
            if gamma_beta is not None:
                gb_ap, gb_res = gamma_beta
                c.step("vector", lambda e, xt=xt, sqt=sqt, mu=mu, rstd=rstd:
                       e.tensor_scalar(sqt[:, :], xt[:, :], mu, rstd, ALU.subtract, ALU.mult),
                       reads=[rxt, rsc], writes=[rsq])
                c.step("vector", lambda e, sqt=sqt:
                       e.tensor_mul(sqt[:, :], sqt[:, :], gb_ap[0]),
                       reads=[rsq, gb_res], writes=[rsq])
                c.step("vector", lambda e, sqt=sqt:
                       e.tensor_add(sqt[:, :], sqt[:, :], gb_ap[1]),
                       reads=[rsq, gb_res], writes=[rsq])
                if dst_f32 is not None:
                    c.step("sync", lambda e, t0=t0, sqt=sqt:
                           e.dma_start(out=dst_f32[t0:t0 + 128, :], in_=sqt[:, :]),
                           reads=[rsq], writes=[dstf_res], dma=True)
                if dst_xn is not None:
                    c.step("vector", lambda e, sqt=sqt, xnt=xnt:
                           e.tensor_copy(xnt[:, 0:D], sqt[:, :]),
                           reads=[rsq], writes=[rxn])
                    c.step("sync", lambda e, t0=t0, xnt=xnt:
                           e.dma_start(out=dst_xn[t0:t0 + 128, :], in_=xnt[:, 0:D]),
                           reads=[rxn], writes=[dstxn_res], dma=True)
                continue
            elif not emit_lo:
                c.step("vector", lambda e, xt=xt, xnt=xnt, mu=mu, rstd=rstd:
                       e.tensor_scalar(xnt[:, 0:D], xt[:, :], mu, rstd, ALU.subtract, ALU.mult),
                       reads=[rxt, rsc], writes=[rxn])
            else:
                xnf = sqt
                c.step("vector", lambda e, xt=xt, xnf=xnf, mu=mu, rstd=rstd:
                       e.tensor_scalar(xnf[:, :], xt[:, :], mu, rstd, ALU.subtract, ALU.mult),
                       reads=[rxt, rsc], writes=[rsq])
                c.step("vector", lambda e, xnf=xnf, xnt=xnt: e.tensor_copy(xnt[:, 0:D], xnf[:, :]),
                       reads=[rsq], writes=[rxn])
                c.step("vector", lambda e, xnf=xnf, xnt=xnt:
                       e.tensor_sub(xnt[:, D:2 * D], xnf[:, :], xnt[:, 0:D]),
                       reads=[rsq, rxn], writes=[rxn])
                c.step("sync", lambda e, t0=t0, xnt=xnt:
                       e.dma_start(out=self.xnlo_d[0][t0:t0 + 128, :], in_=xnt[:, D:2 * D]),
                       reads=[rxn], writes=[self.xnlo_d[1]], dma=True)
                if emit_f32:
                    c.step("sync", lambda e, t0=t0, xnf=xnf:
                           e.dma_start(out=self.xnf_d[0][t0:t0 + 128, :], in_=xnf[:, :]),
                           reads=[rsq], writes=[self.xnf_d[1]], dma=True)
            c.step("sync", lambda e, t0=t0, xnt=xnt: e.dma_start(out=dst_xn[t0:t0 + 128, :], in_=xnt[:, 0:D]),
                   reads=[rxn], writes=[dstxn_res], dma=True)

    def emit_xnT(self, src_dram, src_res):
        c = self.c
        for c8 in range(ND):
            c.step("scalar", lambda e, c8=c8:
                   e.dma_start_transpose(self.big8[:, 4096 * c8:4096 * (c8 + 1)],
                                         src_dram[:, 128 * c8:128 * (c8 + 1)]),
                   reads=[src_res], writes=[self.r_big8], dma=True)

    def big8v(self, c8):
        return self.big8[:, 4096 * c8:4096 * (c8 + 1)]

    def big8h(self, hc):
        return self.big8[:, 1024 * hc:1024 * (hc + 1)]

    def load_wslab8(self, scr_ap, scr_res, ig, n_in=1024, mode="xbar"):
        c = self.c
        for d in range(n_in // 128):
            if mode == "xbar":
                c.step("scalar", lambda e, d=d, ig=ig:
                       e.dma_start_transpose(self.wslab8[:, 1024 * d:1024 * (d + 1)],
                                             scr_ap[1024 * ig:1024 * (ig + 1), 128 * d:128 * (d + 1)]),
                       reads=[scr_res], writes=[self.r_wslab8], dma=True)
            else:
                c.step("sync", lambda e, d=d, ig=ig:
                       e.dma_start(out=self.wslab8[:, 1024 * d:1024 * (d + 1)],
                                   in_=scr_ap[128 * d:128 * (d + 1), 1024 * ig:1024 * (ig + 1)]),
                       reads=[scr_res], writes=[self.r_wslab8], dma=True)

    def emit_proj_fT(self, scr_ap, scr_res, n_out, sink, bias_col=None, nin=1024):
        c = self.c
        ndc = nin // 128
        for ig in range(n_out // 1024):
            self.load_wslab8(scr_ap, scr_res, ig, n_in=nin)
            for il in range(8):
                i128 = ig * 8 + il
                for d in range(ndc):
                    for tb in range(8):
                        c.step("tensor", lambda e, d=d, il=il, tb=tb:
                               e.matmul(self.psv(tb), self.wslab8[:, 1024 * d + 128 * il:1024 * d + 128 * (il + 1)],
                                        self.big8v(d)[:, 512 * tb:512 * (tb + 1)],
                                        start=(d == 0), stop=(d == ndc - 1)),
                               reads=[self.r_wslab8, self.r_big8], writes=[self.r_ps[tb]])
                sink(i128)

    def emit_final(self, src, src_res, adds, add_res):
        self.emit_ln_merge(src, adds, self.out_ext, None, src_res=src_res,
                           add_res=add_res, dstr_res=self.r_out)

    def finish(self):
        c = self.c
        for s in c._bufstreams.values():
            if s.count:
                c.barrier_wait("sync", _Ev("dma", sem=s.sem, val=16 * s.count))
        block = self.stack.enter_context(self.nc.Block())
        c.finalize(block)
        self.stack.close()
        return self.nc

    def emit_quantize(self, w_ap, rows, cols, dst_ap, dst_res, tag):
        c = self.c
        nb_r, nb_c = rows // 128, cols // 1024
        acc = self.scol[:, 120:121]
        tmp = self.scol[:, 121:122]
        thr = self.scol[:, 122:123]
        racc = self.r_scol[15]
        c.step("vector", lambda e: e.memset(acc, 0.0), writes=[racc])
        blocks = [(r, cc) for r in range(nb_r) for cc in range(nb_c)]
        for bi, (r, cc) in enumerate(blocks):
            p = bi % 2
            wt, rwt = self.pf[p], self.r_pf[p]
            c.step("sync", lambda e, r=r, cc=cc, wt=wt:
                   e.dma_start(out=wt[:, :], in_=w_ap[128 * r:128 * (r + 1), 1024 * cc:1024 * (cc + 1)]),
                   writes=[rwt], dma=True)
            c.step("vector", lambda e, wt=wt:
                   e.tensor_reduce(tmp, wt[:, :], axis=AX.X, op=ALU.add, apply_absolute_value=True),
                   reads=[rwt], writes=[racc])
            c.step("vector", lambda e: e.tensor_add(acc, acc, tmp), reads=[racc], writes=[racc])
        c.step("tensor", lambda e: e.matmul(self.psA[0:1, 0:1], acc, self.onescol[:, 0:1],
                                            start=True, stop=True),
               reads=[racc, self.r_const], writes=[self.r_ps[0]])
        c.step("vector", lambda e: e.tensor_scalar(self.scol[0:1, 123:124], self.psA[0:1, 0:1],
                                                   0.5 / (rows * cols), 0.5e-6, ALU.mult, ALU.add),
               reads=[self.r_ps[0]], writes=[racc])
        c.step("tensor", lambda e: e.matmul(self.psA[:, 4:5], self.onesrow_f[0:1, :],
                                            self.scol[0:1, 123:124], start=True, stop=True),
               reads=[racc, self.r_const], writes=[self.r_ps[0]])
        c.step("vector", lambda e: e.tensor_copy(thr, self.psA[:, 4:5]),
               reads=[self.r_ps[0]], writes=[racc])
        for bi, (r, cc) in enumerate(blocks):
            p = bi % 2
            wt, rwt = self.pf[p], self.r_pf[p]
            at, rat = self.pf[2 + p], self.r_pf[2 + p]
            qt = self.pbf[2]
            rqt = self.r_pbf[2]
            qs = qt[:, 1024 * p:1024 * (p + 1)]
            c.step("sync", lambda e, r=r, cc=cc, wt=wt:
                   e.dma_start(out=wt[:, :], in_=w_ap[128 * r:128 * (r + 1), 1024 * cc:1024 * (cc + 1)]),
                   writes=[rwt], dma=True)
            c.step("scalar", lambda e, wt=wt, at=at:
                   e.activation(out=at[:, :], in_=wt[:, :], func=AF.Abs),
                   reads=[rwt], writes=[rat])
            c.step("vector", lambda e, at=at:
                   e.tensor_scalar(at[:, :], at[:, :], thr, None, ALU.is_gt),
                   reads=[rat, racc], writes=[rat])
            c.step("scalar", lambda e, wt=wt, qs=qs: e.sign(qs, wt[:, :]),
                   reads=[rwt], writes=[rqt])
            c.step("vector", lambda e, at=at, qs=qs: e.tensor_mul(qs, qs, at[:, :]),
                   reads=[rat], writes=[rqt])
            c.step("sync", lambda e, r=r, cc=cc, qs=qs:
                   e.dma_start(out=dst_ap[128 * r:128 * (r + 1), 1024 * cc:1024 * (cc + 1)], in_=qs),
                   reads=[rqt], writes=[dst_res], dma=True)

    def emit_proj_tok(self, statv, stat_res, kchunks, wload, n_out, sink,
                      resident=False, start_k=0):
        c = self.c
        if resident:
            for j0 in range(0, n_out, 512):
                for toct in range(NT // 8):
                    for kc in range(kchunks):
                        ap, rs = wload(kc)
                        for t in range(8):
                            t128 = toct * 8 + t
                            c.step("tensor", lambda e, ap=ap, kc=kc, t128=t128, j0=j0, t=t:
                                   e.matmul(self.psv(t), statv(kc)[:, 128 * t128:128 * (t128 + 1)],
                                            ap[:, j0:j0 + 512],
                                            start=(kc == start_k), stop=(kc == kchunks - 1)),
                                   reads=[rs, stat_res], writes=[self.r_ps[t]])
                    for t in range(8):
                        sink(toct * 8 + t, j0, self.psv(t), [self.r_ps[t]])
        else:
            for j0 in range(0, n_out, 512):
                for toct in range(NT // 8):
                    for kc in range(kchunks):
                        ap, rs = wload(kc, j0, kc % 2)
                        for t in range(8):
                            t128 = toct * 8 + t
                            c.step("tensor", lambda e, ap=ap, kc=kc, t128=t128, t=t:
                                   e.matmul(self.psv(t), statv(kc)[:, 128 * t128:128 * (t128 + 1)],
                                            ap, start=(kc == start_k), stop=(kc == kchunks - 1)),
                                   reads=[rs, stat_res], writes=[self.r_ps[t]])
                    for t in range(8):
                        sink(toct * 8 + t, j0, self.psv(t), [self.r_ps[t]])

    def wload_xbar_stream(self, scr_ap, scr_res):
        c = self.c

        def load(kc, j0, parity):
            ap = self.wst[parity][:, 0:512]
            c.step("scalar", lambda e, kc=kc, j0=j0, ap=ap:
                   e.dma_start_transpose(ap, scr_ap[j0:j0 + 512, 128 * kc:128 * (kc + 1)]),
                   reads=[scr_res], writes=[self.r_wst[parity]], dma=True)
            return ap, self.r_wst[parity]
        return load

    def wload_plain_stream(self, src_ap, src_res, cast=False):
        c = self.c

        def load(kc, j0, parity):
            ap = self.wst[parity][:, 0:512]
            eng = "gpsimd" if cast else "sync"
            c.step(eng, lambda e, kc=kc, j0=j0, ap=ap:
                   e.dma_start(out=ap, in_=src_ap[128 * kc:128 * (kc + 1), j0:j0 + 512]),
                   reads=[src_res], writes=[self.r_wst[parity]], dma=True)
            return ap, self.r_wst[parity]
        return load

    def load_wslab8_resident(self, src_ap, src_res, n_out=1024, mode="xbar"):
        c = self.c
        for kc in range(8):
            dst = self.wslab8[:, 1024 * kc:1024 * kc + n_out]
            if mode == "xbar":
                c.step("scalar", lambda e, kc=kc, dst=dst:
                       e.dma_start_transpose(dst, src_ap[0:n_out, 128 * kc:128 * (kc + 1)]),
                       reads=[src_res], writes=[self.r_wslab8], dma=True)
            elif mode == "plain":
                c.step("sync", lambda e, kc=kc, dst=dst:
                       e.dma_start(out=dst, in_=src_ap[128 * kc:128 * (kc + 1), 0:n_out]),
                       reads=[src_res], writes=[self.r_wslab8], dma=True)
            else:
                c.step("gpsimd", lambda e, kc=kc, dst=dst:
                       e.dma_start(out=dst, in_=src_ap[128 * kc:128 * (kc + 1), 0:n_out]),
                       reads=[src_res] if src_res is not None else [],
                       writes=[self.r_wslab8], dma=True)

        def wl(kc):
            return self.wslab8[:, 1024 * kc:1024 * (kc + 1)], self.r_wslab8
        return wl

    def emit_rope_consts(self):
        c = self.c
        g = lambda fn: c.step("gpsimd", fn, writes=[self.r_const])
        p2 = self.prot[:, 0:128]
        g(lambda e: e.memset(p2, 0.0))
        for half in (0, 64):
            g(lambda e, h=half: e.affine_select(
                out=self.prot[h:h + 64, 0 + h:32 + h], in_=self.prot[h:h + 64, 0 + h:32 + h],
                pattern=[[-2, 32]], compare_op=ALU.not_equal, fill=-1.0,
                base=-1, channel_multiplier=1))
            g(lambda e, h=half: e.affine_select(
                out=self.prot[h:h + 64, 32 + h:64 + h], in_=self.prot[h:h + 64, 32 + h:64 + h],
                pattern=[[-2, 32]], compare_op=ALU.not_equal, fill=1.0,
                base=0, channel_multiplier=1))
        p3 = self.prot[:, 128:256]
        g(lambda e: e.memset(p3, 0.0))
        for half in (0, 64):
            g(lambda e, h=half: e.affine_select(
                out=self.prot[h:h + 64, 128 + h:160 + h], in_=self.prot[h:h + 64, 128 + h:160 + h],
                pattern=[[-1, 32]], compare_op=ALU.not_equal, fill=-1.0,
                base=-32, channel_multiplier=1))
            g(lambda e, h=half: e.affine_select(
                out=self.prot[h:h + 64, 160 + h:192 + h], in_=self.prot[h:h + 64, 160 + h:192 + h],
                pattern=[[-1, 32]], compare_op=ALU.not_equal, fill=1.0,
                base=0, channel_multiplier=1))

    def emit_cos_sin(self, flavor):
        c = self.c
        base_pos = 0 if flavor == 2 else 1
        ivf = self.scol[:, 124:125]
        rsc = self.r_scol[14]
        c.step("gpsimd", lambda e: e.iota(ivf, pattern=[[0, 1]], base=0, channel_multiplier=1, allow_small_or_imprecise_dtypes=True),
               writes=[rsc])
        tmpc = self.scol[:, 125:126]
        c.step("vector", lambda e: e.tensor_scalar(tmpc, ivf, 1.0 / 32.0, 15.5 / 32.0, ALU.mult, ALU.subtract),
               reads=[rsc], writes=[rsc])
        c.step("vector", lambda e: e.tensor_scalar(tmpc, tmpc, MAGIC, MAGIC, ALU.add, ALU.subtract),
               reads=[rsc], writes=[rsc])
        c.step("vector", lambda e: e.scalar_tensor_tensor(ivf, tmpc, -32.0, ivf, ALU.mult, ALU.add),
               reads=[rsc], writes=[rsc])
        c.step("scalar", lambda e: e.activation(out=ivf, in_=ivf, func=AF.Exp,
                                                scale=-math.log(10000.0) / 32.0),
               reads=[rsc], writes=[rsc])
        TWO_PI = 2.0 * math.pi
        MAGIC = 12582912.0
        for tb in range(8):
            it = self.pf[2][:, 0:512]
            msk = self.pf[3][:, 0:512]
            r1, r2 = self.r_pf[2], self.r_pf[3]
            c.step("gpsimd", lambda e, tb=tb: e.iota(it, pattern=[[1, 512]],
                                                     base=base_pos + 512 * tb, channel_multiplier=0, allow_small_or_imprecise_dtypes=True),
                   writes=[r1])
            c.step("vector", lambda e: e.tensor_scalar(it, it, ivf, None, ALU.mult),
                   reads=[r1, rsc], writes=[r1])
            c.step("vector", lambda e: e.tensor_scalar(msk, it, 1.0 / TWO_PI, None, ALU.mult),
                   reads=[r1], writes=[r2])
            c.step("vector", lambda e: e.tensor_scalar(msk, msk, MAGIC, MAGIC, ALU.add, ALU.subtract),
                   reads=[r2], writes=[r2])
            c.step("vector", lambda e: e.scalar_tensor_tensor(it, msk, -TWO_PI, it, ALU.mult, ALU.add),
                   reads=[r2, r1], writes=[r1])
            c.step("vector", lambda e: e.tensor_scalar(it, it, math.pi, -math.pi, ALU.min, ALU.max),
                   reads=[r1], writes=[r1])
            c.step("scalar", lambda e, tb=tb: e.activation(out=self.pbf[1][:, 512 * tb:512 * (tb + 1)],
                                                           in_=it, func=AF.Sin),
                   reads=[r1], writes=[self.r_pbf[1]])
            c.step("scalar", lambda e: e.activation(out=msk, in_=it, func=AF.Abs),
                   reads=[r1], writes=[r2])
            c.step("scalar", lambda e, tb=tb: e.activation(out=self.pbf[0][:, 512 * tb:512 * (tb + 1)],
                                                           in_=msk, func=AF.Sin,
                                                           scale=-1.0, bias=math.pi / 2.0),
                   reads=[r2, self.r_const], writes=[self.r_pbf[0]])

    def emit_qk_proj(self, wname, dst_d, dst_res, flavor, bias_name=None, bias_col0=128):
        c = self.c
        scr_ap, scr_res = self.wscr[wname]
        prot = self.prot[:, 0:128] if flavor == 2 else self.prot[:, 128:256]
        bias_cols = None
        if bias_name is not None:
            bias_cols = self.load_vec_cols(self.W[bias_name], 1024, bias_col0)

        def sink(i128):
            qt = self.pbf[2]
            rqt = self.r_pbf[2]
            for tb in range(8):
                if bias_cols is None:
                    c.step("vector", lambda e, tb=tb, qt=qt:
                           e.tensor_copy(qt[:, 512 * tb:512 * (tb + 1)], self.psv(tb)),
                           reads=[self.r_ps[tb]], writes=[rqt])
                else:
                    bc = bias_cols[:, bias_col0 + i128 // 128:bias_col0 + i128 // 128 + 1]
                    c.step("vector", lambda e, tb=tb, qt=qt, bc=bc:
                           e.tensor_scalar(qt[:, 512 * tb:512 * (tb + 1)], self.psv(tb),
                                           bc, None, ALU.add),
                           reads=[self.r_ps[tb], self.r_fstat], writes=[rqt])
            for tb in range(8):
                pv = self.psv(tb)
                c.step("tensor", lambda e, tb=tb, qt=qt, pv=pv:
                       e.matmul(pv, prot, qt[:, 512 * tb:512 * (tb + 1)], start=True, stop=True),
                       reads=[rqt, self.r_const], writes=[self.r_ps[tb]])
                tmp = self.wst[0][:, 0:512]
                c.step("vector", lambda e, tb=tb, qt=qt:
                       e.tensor_mul(tmp, qt[:, 512 * tb:512 * (tb + 1)],
                                    self.pbf[0][:, 512 * tb:512 * (tb + 1)]),
                       reads=[rqt, self.r_pbf[0]], writes=[self.r_wst[0]])
                tmp2 = self.wst[1][:, 0:512]
                c.step("vector", lambda e, tb=tb, pv=pv:
                       e.tensor_mul(tmp2, pv, self.pbf[1][:, 512 * tb:512 * (tb + 1)]),
                       reads=[self.r_ps[tb], self.r_pbf[1]], writes=[self.r_wst[1]])
                out_t = qt[:, 512 * tb:512 * (tb + 1)]
                c.step("vector", lambda e, out_t=out_t:
                       e.tensor_add(out_t, tmp, tmp2),
                       reads=[self.r_wst[0], self.r_wst[1]], writes=[rqt])
                c.step("sync", lambda e, i128=i128, tb=tb, out_t=out_t:
                       e.dma_start(out=dst_d[128 * i128:128 * (i128 + 1), 512 * tb:512 * (tb + 1)],
                                   in_=out_t),
                       reads=[rqt], writes=[dst_res], dma=True)
        self.emit_proj_fT(scr_ap, scr_res, 1024, sink)

    def load_vec_cols(self, vec_ap, n, col0, vec_res=None, scale=None):
        c = self.c
        ncol = n // 128
        buf = self.pf[3]
        c.step("sync", lambda e: e.dma_start(out=buf[0:ncol, 0:128],
                                             in_=vec_ap.rearrange("(c p) -> c p", p=128)),
               reads=[vec_res] if vec_res else [], writes=[self.r_pf[3]], dma=True)
        pv = self.psv(6)[:, 0:128]
        c.step("tensor", lambda e: e.transpose(pv, buf[0:128, 0:128], self.ident_f[:, :]),
               reads=[self.r_pf[3], self.r_const], writes=[self.r_ps[6]])
        dst = self.fstat[:, col0:col0 + ncol]
        if scale is None:
            c.step("vector", lambda e: e.tensor_copy(dst, pv[:, 0:ncol]),
                   reads=[self.r_ps[6]], writes=[self.r_fstat])
        else:
            c.step("vector", lambda e: e.tensor_scalar_mul(dst, pv[:, 0:ncol], scale),
                   reads=[self.r_ps[6]], writes=[self.r_fstat])
        return self.fstat

    def emit_v_proj(self, wname, bias_name=None, wmode="xbar"):
        c = self.c
        def vones(e):
            view = self.vres[:, :].rearrange("p (s h o) -> p s h o", s=32, h=16)
            return e.memset(view[:, :, :, 64:65], 1.0)
        c.step("gpsimd", vones, writes=[self.r_vres])
        if wmode == "xbar":
            scr_ap, scr_res = self.wscr[wname]
        else:
            scr_ap, scr_res = self.W[wname].rearrange("d h k -> d (h k)"), None
        wl = self.load_wslab8_resident(scr_ap, scr_res, 1024, mode=wmode)
        if bias_name is not None:
            c.step("sync", lambda e: e.dma_start(out=self.row1k[0:1, :],
                                                 in_=self.W[bias_name].rearrange("(a b) -> a b", a=1)),
                   writes=[self.r_row1k], dma=True)
            for j2 in range(2):
                c.step("tensor", lambda e, j2=j2:
                       e.matmul(self.psv(4 + j2), self.onesrow_f[0:1, :],
                                self.row1k[0:1, 512 * j2:512 * (j2 + 1)], start=True, stop=True),
                       reads=[self.r_row1k, self.r_const], writes=[self.r_ps[4 + j2]])
                c.step("vector", lambda e, j2=j2:
                       e.tensor_copy(self.pf[3][:, 512 * j2:512 * (j2 + 1)], self.psv(4 + j2)),
                       reads=[self.r_ps[4 + j2]], writes=[self.r_pf[3]])

        def sink(t128, j0, pv, rds):
            dst = self.vres[:, 1040 * t128 + (j0 // 512) * 520:
                            1040 * t128 + (j0 // 512) * 520 + 520]
            dstv = dst.rearrange("p (h o) -> p h o", h=8)[:, :, 0:64]
            srcv = pv.rearrange("p (h o) -> p h o", h=8)
            if bias_name is None:
                c.step("vector", lambda e, dstv=dstv, srcv=srcv: e.tensor_copy(dstv, srcv),
                       reads=rds, writes=[self.r_vres])
            else:
                bv = self.pf[3][:, j0:j0 + 512].rearrange("p (h o) -> p h o", h=8)
                c.step("vector", lambda e, dstv=dstv, srcv=srcv, bv=bv:
                       e.tensor_add(dstv, srcv, bv),
                       reads=rds + [self.r_pf[3]], writes=[self.r_vres])
        self.emit_proj_tok(lambda kc: self.big8v(kc), self.r_big8, 8, wl, 1024, sink,
                           resident=True)

    def emit_attn_core(self, causal, qsrc, qres, ksrc, kres):
        c = self.c
        kp, rkp = self.pbf[0], self.r_pbf[0]
        qp, rqp = self.pbf[1], self.r_pbf[1]
        rb = self.r_scol[12]
        rc2 = self.r_scol[4]
        rn = self.r_scol[6]
        TMP = self.scol[:, 102:103]
        c.step("vector", lambda e: e.memset(self.scol[0:1, 48:80], 0.0), writes=[rn])
        for hp in range(8):
            c.step("sync", lambda e, hp=hp: e.dma_start(out=kp[:, :], in_=ksrc[128 * hp:128 * (hp + 1), :]),
                   reads=[kres], writes=[rkp], dma=True)
            c.step("sync", lambda e, hp=hp: e.dma_start(out=qp[:, :], in_=qsrc[128 * hp:128 * (hp + 1), :]),
                   reads=[qres], writes=[rqp], dma=True)
            for ti, (tile, rtile) in enumerate(((kp, rkp), (qp, rqp))):
                for half in range(2):
                    sq = self.wst[half][:, 0:2048]
                    c.step("scalar", lambda e, half=half, sq=sq, tile=tile:
                           e.activation(out=sq, in_=tile[:, 2048 * half:2048 * (half + 1)], func=AF.Square),
                           reads=[rtile], writes=[self.r_wst[half]])
                    for sub in range(4):
                        for hh in range(2):
                            i = 2 * hh + (sub % 2)
                            pv = self.psv(i)
                            col = self.scol[0:1, 48 + 16 * ti + 2 * hp + hh:48 + 16 * ti + 2 * hp + hh + 1]
                            c.step("tensor", lambda e, sq=sq, sub=sub, hh=hh, pv=pv:
                                   e.matmul(pv[0:1, :], self.onescol_bf[64 * hh:64 * (hh + 1), 0:1],
                                            sq[64 * hh:64 * (hh + 1), 512 * sub:512 * (sub + 1)],
                                            start=True, stop=True),
                                   reads=[self.r_wst[half], self.r_const], writes=[self.r_ps[i]])
                            c.step("vector", lambda e, pv=pv: e.reduce_max(TMP[0:1, :], pv[0:1, :], axis=AX.X),
                                   reads=[self.r_ps[i]], writes=[rb])
                            c.step("vector", lambda e, col=col: e.tensor_max(col, col, TMP[0:1, :]),
                                   reads=[rb], writes=[rn])
        for h in range(16):
            kcol = self.scol[0:1, 48 + h:49 + h]
            qcol = self.scol[0:1, 64 + h:65 + h]
            t1 = self.scol[0:1, 103:104]
            c.step("vector", lambda e, kcol=kcol, qcol=qcol, t1=t1: e.tensor_mul(t1, kcol, qcol),
                   reads=[rn], writes=[rb])
            c.step("scalar", lambda e, t1=t1: e.activation(out=t1, in_=t1, func=AF.Sqrt),
                   reads=[rb], writes=[rb])
            c.step("vector", lambda e, t1=t1: e.tensor_scalar_mul(t1, t1, -0.125),
                   reads=[rb], writes=[rb])
            c.step("tensor", lambda e, t1=t1: e.matmul(self.psv(7)[:, 0:1], self.onesrow_f[0:1, :], t1,
                                                       start=True, stop=True),
                   reads=[rb, self.r_const], writes=[self.r_ps[7]])
            c.step("vector", lambda e, h=h: e.tensor_copy(self.scol[:, 32 + h:33 + h], self.psv(7)[:, 0:1]),
                   reads=[self.r_ps[7]], writes=[rc2])

        for hp in range(8):
            c.step("sync", lambda e, hp=hp: e.dma_start(out=kp[:, :], in_=ksrc[128 * hp:128 * (hp + 1), :]),
                   reads=[kres], writes=[rkp], dma=True)
            c.step("sync", lambda e, hp=hp: e.dma_start(out=qp[:, :], in_=qsrc[128 * hp:128 * (hp + 1), :]),
                   reads=[qres], writes=[rqp], dma=True)
            for hh in range(2):
                h = 2 * hp + hh
                khead = kp[64 * hh:64 * (hh + 1), :]
                qhead = qp[64 * hh:64 * (hh + 1), :]
                for tg in range(2):
                    t0 = 2048 * tg
                    sc_hi = 32 if not causal else min(32, (t0 + 2047) // 128 + 1)
                    pt = self.pbf[2][:, 2048 * (tg % 2):2048 * (tg % 2 + 1)]
                    rpt = self.r_pbf[2]
                    for sc in range(sc_hi):
                        for j in range(4):
                            c.step("tensor", lambda e, sc=sc, j=j, khead=khead, qhead=qhead, t0=t0:
                                   e.matmul(self.psv(j), khead[:, 128 * sc:128 * (sc + 1)],
                                            qhead[:, t0 + 512 * j:t0 + 512 * (j + 1)],
                                            start=True, stop=True),
                                   reads=[rkp, rqp], writes=[self.r_ps[j]])
                        negc = self.scol[:, 32 + h:33 + h]
                        c.step("scalar", lambda e, pt=pt, negc=negc:
                               e.activation(out=pt, in_=self.psA[:, 0:2048], func=AF.Exp,
                                            scale=0.125, bias=negc),
                               reads=[self.r_ps[0], self.r_ps[1], self.r_ps[2], self.r_ps[3], rc2],
                               writes=[rpt])
                        if causal and 128 * sc + 127 >= t0:
                            c.step("gpsimd", lambda e, pt=pt, sc=sc, t0=t0:
                                   e.affine_select(out=pt, in_=pt, pattern=[[1, 2048]],
                                                   compare_op=ALU.is_ge, fill=0.0,
                                                   base=t0 - 128 * sc, channel_multiplier=-1),
                                   reads=[rpt], writes=[rpt])
                        vblk = self.vres[:, 1040 * sc + 65 * h:1040 * sc + 65 * (h + 1)]
                        for j in range(4):
                            c.step("tensor", lambda e, sc=sc, j=j, vblk=vblk, pt=pt, sc_hi=sc_hi:
                                   e.matmul(self.psB[0:65, 512 * j:512 * (j + 1)], vblk,
                                            pt[:, 512 * j:512 * (j + 1)],
                                            start=(sc == 0), stop=(sc == sc_hi - 1)),
                                   reads=[self.r_vres, rpt], writes=[self.r_ps[4 + j]])
                    for half in range(2):
                        o = 1024 * half
                        c.step("vector", lambda e, o=o: e.reciprocal(self.row1k[0:1, :],
                                                                     self.psB[64:65, o:o + 1024]),
                               reads=[self.r_ps[4 + 2 * half], self.r_ps[5 + 2 * half]],
                               writes=[self.r_row1k])
                        for j2 in range(2):
                            c.step("tensor", lambda e, o=o, j2=j2:
                                   e.matmul(self.psv(j2)[0:64, :], self.onesrow_f[0:1, 0:64],
                                            self.row1k[0:1, 512 * j2:512 * (j2 + 1)],
                                            start=True, stop=True),
                                   reads=[self.r_row1k, self.r_const], writes=[self.r_ps[j2]])
                        nb = self.pf[2][0:64, :]
                        c.step("vector", lambda e, o=o, nb=nb:
                               e.tensor_copy(nb, self.psB[0:64, o:o + 1024]),
                               reads=[self.r_ps[4 + 2 * half], self.r_ps[5 + 2 * half]],
                               writes=[self.r_pf[2]])
                        dst = self.big8v(hp)[64 * hh:64 * (hh + 1), t0 + o:t0 + o + 1024]
                        c.step("vector", lambda e, dst=dst, nb=nb:
                               e.tensor_mul(dst, nb, self.psA[0:64, 0:1024]),
                               reads=[self.r_pf[2], self.r_ps[0], self.r_ps[1]],
                               writes=[self.r_big8])

    def emit_out_proj(self, scr_name, dst_d, dst_res, bias_name=None, add_bias_col=None):
        c = self.c
        scr_ap, scr_res = self.wscr[scr_name]
        wl = self.load_wslab8_resident(scr_ap, scr_res, 1024, mode="xbar")
        bias_tile = None
        if bias_name is not None:
            c.step("sync", lambda e: e.dma_start(out=self.row1k[0:1, :], in_=self.W[bias_name].rearrange('(a b) -> a b', a=1)),
                   writes=[self.r_row1k], dma=True)
            for j2 in range(2):
                c.step("tensor", lambda e, j2=j2:
                       e.matmul(self.psv(4 + j2), self.onesrow_f[0:1, :],
                                self.row1k[0:1, 512 * j2:512 * (j2 + 1)], start=True, stop=True),
                       reads=[self.r_row1k, self.r_const], writes=[self.r_ps[4 + j2]])
            for j2 in range(2):
                c.step("vector", lambda e, j2=j2:
                       e.tensor_copy(self.pf[3][:, 512 * j2:512 * (j2 + 1)], self.psv(4 + j2)),
                       reads=[self.r_ps[4 + j2]], writes=[self.r_pf[3]])
            bias_tile = self.pf[3]

        def sink(t128, j0, pv, rds):
            ot = self.pf[2][:, 512 * (t128 % 2):512 * (t128 % 2 + 1)]
            if bias_tile is None:
                c.step("vector", lambda e, pv=pv, ot=ot: e.tensor_copy(ot, pv),
                       reads=rds, writes=[self.r_pf[2]])
            else:
                c.step("vector", lambda e, pv=pv, ot=ot, j0=j0:
                       e.tensor_add(ot, pv, bias_tile[:, j0:j0 + 512]),
                       reads=rds + [self.r_pf[3]], writes=[self.r_pf[2]])
            c.step("sync", lambda e, t128=t128, j0=j0, ot=ot:
                   e.dma_start(out=dst_d[128 * t128:128 * (t128 + 1), j0:j0 + 512], in_=ot),
                   reads=[self.r_pf[2]], writes=[dst_res], dma=True)
        self.emit_proj_tok(lambda kc: self.big8v(kc), self.r_big8, 8, wl, 1024, sink,
                           resident=True)

    def emit_l2_ffn(self):
        c = self.c
        rstat = self.r_fstat
        c.step("vector", lambda e: e.memset(self.fstat[:, 0:64], 0.0), writes=[rstat])
        wl = self.wload_xbar_stream(self.wfcq_d[0], self.wfcq_d[1])

        def sink1(t128, j0, pv, rds):
            ht = self.pbf[2][:, 512 * (t128 % 2):512 * (t128 % 2 + 1)]
            dump = self.pbf[2][:, 2048:2560]
            s1p = self.scol[:, 110:111]
            s2p = self.scol[:, 111:112]
            rs = self.r_scol[11]
            c.step("scalar", lambda e, pv=pv, ht=ht, s1p=s1p:
                   e.activation(out=ht, in_=pv, func=AF.Gelu, accum_out=s1p),
                   reads=rds, writes=[self.r_pbf[2], rs])
            c.step("vector", lambda e, ht=ht, s1p=s1p, t128=t128:
                   e.tensor_add(self.fstat[:, t128:t128 + 1], self.fstat[:, t128:t128 + 1], s1p),
                   reads=[rs], writes=[rstat])
            c.step("vector", lambda e, ht=ht, dump=dump, s2p=s2p:
                   e.scalar_tensor_tensor(dump, ht, 1.0, ht, ALU.mult, ALU.mult, accum_out=s2p),
                   reads=[self.r_pbf[2]], writes=[self.r_pbf[2], rs])
            c.step("vector", lambda e, s2p=s2p, t128=t128:
                   e.tensor_add(self.fstat[:, 32 + t128:33 + t128], self.fstat[:, 32 + t128:33 + t128], s2p),
                   reads=[rs], writes=[rstat])
            c.step("sync", lambda e, t128=t128, j0=j0, ht=ht:
                   e.dma_start(out=self.h_d[0][128 * t128:128 * (t128 + 1), j0:j0 + 512], in_=ht),
                   reads=[self.r_pbf[2]], writes=[self.h_d[1]], dma=True)
        self.emit_proj_tok(lambda kc: self.big8v(kc), self.r_big8, 8, wl, 4096, sink1)

        H = 4096.0
        for t128 in range(NT):
            s1 = self.fstat[:, t128:t128 + 1]
            ssq = self.fstat[:, 32 + t128:33 + t128]
            rstd = self.fstat[:, 64 + t128:65 + t128]
            s2 = self.fstat[:, 96 + t128:97 + t128]
            c.step("vector", lambda e, s1=s1: e.tensor_scalar_mul(s1, s1, 1.0 / H),
                   reads=[rstat], writes=[rstat])
            c.step("vector", lambda e, ssq=ssq: e.tensor_scalar_mul(ssq, ssq, 1.0 / H),
                   reads=[rstat], writes=[rstat])
            c.step("vector", lambda e, s1=s1, ssq=ssq:
                   e.scalar_tensor_tensor(ssq, s1, s1, ssq, ALU.mult, ALU.subtract),
                   reads=[rstat], writes=[rstat])
            c.step("scalar", lambda e, ssq=ssq: e.activation(out=ssq, in_=ssq, func=AF.Ln,
                                                             scale=-1.0, bias=1e-5),
                   reads=[rstat, self.r_const], writes=[rstat])
            c.step("scalar", lambda e, ssq=ssq, rstd=rstd:
                   e.activation(out=rstd, in_=ssq, func=AF.Exp, scale=-0.5),
                   reads=[rstat], writes=[rstat])
            c.step("vector", lambda e, s1=s1, rstd=rstd, s2=s2:
                   e.tensor_mul(s2, s1, rstd), reads=[rstat], writes=[rstat])

        self.load_vec_cols(self.W["L2_ffn_ln_w"], 4096, 160)
        self.load_vec_cols(self.W["L2_ffn_ln_b"], 4096, 192)
        gbf = self.pbf[2][:, 3072:3104]
        bbf = self.pbf[2][:, 3104:3136]
        c.step("vector", lambda e: e.tensor_copy(gbf, self.fstat[:, 160:192]),
               reads=[self.r_fstat], writes=[self.r_pbf[2]])
        c.step("vector", lambda e: e.tensor_copy(bbf, self.fstat[:, 192:224]),
               reads=[self.r_fstat], writes=[self.r_pbf[2]])

        for kc in range(32):
            p = kc % 2
            slab = self.wst[p][:, 0:1024]
            c.step("scalar", lambda e, kc=kc, slab=slab:
                   e.dma_start_transpose(slab, self.wprojq_d[0][0:1024, 128 * kc:128 * (kc + 1)]),
                   reads=[self.wprojq_d[1]], writes=[self.r_wst[p]], dma=True)
            for j2 in range(2):
                c.step("tensor", lambda e, kc=kc, slab=slab, j2=j2:
                       e.matmul(self.psv(4 + j2)[0:1, :], gbf[:, kc:kc + 1],
                                slab[:, 512 * j2:512 * (j2 + 1)],
                                start=(kc == 0), stop=(kc == 31)),
                       reads=[self.r_wst[p], self.r_pbf[2]], writes=[self.r_ps[4 + j2]])
                c.step("tensor", lambda e, kc=kc, slab=slab, j2=j2:
                       e.matmul(self.psv(6 + j2)[0:1, :], bbf[:, kc:kc + 1],
                                slab[:, 512 * j2:512 * (j2 + 1)],
                                start=(kc == 0), stop=(kc == 31)),
                       reads=[self.r_wst[p], self.r_pbf[2]], writes=[self.r_ps[6 + j2]])
        for j2 in range(2):
            c.step("vector", lambda e, j2=j2: e.tensor_copy(self.row1k[0:1, 512 * j2:512 * (j2 + 1)],
                                                            self.psv(4 + j2)[0:1, :]),
                   reads=[self.r_ps[4 + j2]], writes=[self.r_row1k])
        for j2 in range(2):
            c.step("tensor", lambda e, j2=j2: e.matmul(self.psv(j2), self.onesrow_f[0:1, :],
                                                       self.row1k[0:1, 512 * j2:512 * (j2 + 1)],
                                                       start=True, stop=True),
                   reads=[self.r_row1k, self.r_const], writes=[self.r_ps[j2]])
            c.step("vector", lambda e, j2=j2: e.tensor_copy(self.pf[0][:, 512 * j2:512 * (j2 + 1)],
                                                            self.psv(j2)),
                   reads=[self.r_ps[j2]], writes=[self.r_pf[0]])
        for j2 in range(2):
            c.step("vector", lambda e, j2=j2: e.tensor_copy(self.row1k[0:1, 512 * j2:512 * (j2 + 1)],
                                                            self.psv(6 + j2)[0:1, :]),
                   reads=[self.r_ps[6 + j2]], writes=[self.r_row1k])
        for j2 in range(2):
            c.step("tensor", lambda e, j2=j2: e.matmul(self.psv(j2), self.onesrow_f[0:1, :],
                                                       self.row1k[0:1, 512 * j2:512 * (j2 + 1)],
                                                       start=True, stop=True),
                   reads=[self.r_row1k, self.r_const], writes=[self.r_ps[j2]])
            c.step("vector", lambda e, j2=j2: e.tensor_copy(self.pf[1][:, 512 * j2:512 * (j2 + 1)],
                                                            self.psv(j2)),
                   reads=[self.r_ps[j2]], writes=[self.r_pf[1]])

        for tgp in range(4):
            trow0 = 1024 * tgp
            for hc in range(32):
                c.step("scalar", lambda e, hc=hc, trow0=trow0:
                       e.dma_start_transpose(self.big8h(hc),
                                             self.h_d[0][trow0:trow0 + 1024, 128 * hc:128 * (hc + 1)]),
                       reads=[self.h_d[1]], writes=[self.r_big8], dma=True)
                c.step("vector", lambda e, hc=hc:
                       e.tensor_scalar(self.big8h(hc), self.big8h(hc), self.fstat[:, 160 + hc:161 + hc],
                                       None, ALU.mult),
                       reads=[self.r_fstat, self.r_big8], writes=[self.r_big8])
            for j0 in (0, 512):
                for tq in range(2):
                    for kc in range(32):
                        p = kc % 2
                        slab = self.wst[p][:, 0:512]
                        c.step("scalar", lambda e, kc=kc, j0=j0, slab=slab:
                               e.dma_start_transpose(slab, self.wprojq_d[0][j0:j0 + 512, 128 * kc:128 * (kc + 1)]),
                               reads=[self.wprojq_d[1]], writes=[self.r_wst[p]], dma=True)
                        for t in range(4):
                            tl = tq * 4 + t
                            bank = 4 * (tq % 2) + t
                            c.step("tensor", lambda e, kc=kc, tl=tl, slab=slab, bank=bank:
                                   e.matmul(self.psv(bank), self.big8h(kc)[:, 128 * tl:128 * (tl + 1)],
                                            slab, start=(kc == 0), stop=(kc == 31)),
                                   reads=[self.r_wst[p], self.r_big8], writes=[self.r_ps[bank]])
                    for t in range(4):
                        tl = tq * 4 + t
                        t128 = 8 * tgp + tl
                        bank = 4 * (tq % 2) + t
                        pv = self.psv(bank)
                        s1 = self.fstat[:, 64 + t128:65 + t128]
                        s2 = self.fstat[:, 96 + t128:97 + t128]
                        tmp = self.pf[2][:, 512 * (t % 2):512 * (t % 2 + 1)]
                        ot = self.pf[3][:, 512 * (t % 2):512 * (t % 2 + 1)]
                        c.step("vector", lambda e, j0=j0, s2=s2, tmp=tmp:
                               e.scalar_tensor_tensor(tmp, self.pf[0][:, j0:j0 + 512], s2,
                                                      self.pf[1][:, j0:j0 + 512], ALU.mult, ALU.subtract),
                               reads=[self.r_pf[0], self.r_pf[1], self.r_fstat], writes=[self.r_pf[2]])
                        c.step("vector", lambda e, pv=pv, s1=s1, tmp=tmp, ot=ot:
                               e.scalar_tensor_tensor(ot, pv, s1, tmp, ALU.mult, ALU.subtract),
                               reads=[self.r_ps[bank], self.r_pf[2], self.r_fstat], writes=[self.r_pf[3]])
                        c.step("sync", lambda e, t128=t128, j0=j0, ot=ot:
                               e.dma_start(out=self.fout_d[0][128 * t128:128 * (t128 + 1), j0:j0 + 512],
                                           in_=ot),
                               reads=[self.r_pf[3]], writes=[self.fout_d[1]], dma=True)

    def emit_router(self, n_e, sink):
        c = self.c
        lo = self.pbf[2]
        rlo = self.r_pbf[2]
        for toct in range(4):
            for d in range(ND):
                c.step("scalar", lambda e, d=d:
                       e.dma_start_transpose(lo[:, 0:4096], self.xnlo_d[0][:, 128 * d:128 * (d + 1)]),
                       reads=[self.xnlo_d[1]], writes=[rlo], dma=True)
                for t in range(8):
                    t128 = 8 * toct + t
                    ts0 = 128 * t128
                    pv = self.psv(t)[:, 0:8]
                    whi = self.weff_bf[:, 8 * d:8 * d + n_e] if n_e == 8 else None
                    wlo = self.weff_bf[:, 64 + 8 * d:64 + 8 * d + n_e]
                    c.step("tensor", lambda e, d=d, ts0=ts0, whi=whi, pv=pv:
                           e.matmul(pv, self.big8v(d)[:, ts0:ts0 + 128], whi,
                                    start=(d == 0), stop=False),
                           reads=[self.r_big8, self.r_weff], writes=[self.r_ps[t]])
                    c.step("tensor", lambda e, d=d, ts0=ts0, wlo=wlo, pv=pv:
                           e.matmul(pv, self.big8v(d)[:, ts0:ts0 + 128], wlo,
                                    start=False, stop=False),
                           reads=[self.r_big8, self.r_weff], writes=[self.r_ps[t]])
                    c.step("tensor", lambda e, d=d, ts0=ts0, whi=whi, pv=pv:
                           e.matmul(pv, lo[:, ts0:ts0 + 128], whi,
                                    start=False, stop=(d == ND - 1)),
                           reads=[rlo, self.r_weff], writes=[self.r_ps[t]])
            for t in range(8):
                t128 = 8 * toct + t
                sink(t128, self.psv(t % 8)[:, 0:8], [self.r_ps[t % 8]])

    def emit_weff_from_natural(self, src_ap, src_res, n_e, hi_lo=True, cast=True):
        c = self.c
        for d in range(ND):
            dst = self.weff[:, 8 * d:8 * d + n_e]
            c.step("gpsimd", lambda e, d=d, dst=dst:
                   e.dma_start(out=self.pf[3][:, 0:n_e], in_=src_ap[128 * d:128 * (d + 1), 0:n_e]),
                   reads=[src_res] if src_res else [], writes=[self.r_pf[3]], dma=True)
            c.step("vector", lambda e, dst=dst: e.tensor_copy(dst, self.pf[3][:, 0:n_e]),
                   reads=[self.r_pf[3]], writes=[self.r_weff])
        self._weff_hilo(n_e)

    def _weff_hilo(self, n_e):
        c = self.c
        for d in range(ND):
            w = self.weff[:, 8 * d:8 * d + n_e]
            hi = self.weff_bf[:, 8 * d:8 * d + n_e]
            lo = self.weff_bf[:, 64 + 8 * d:64 + 8 * d + n_e]
            c.step("vector", lambda e, w=w, hi=hi: e.tensor_copy(hi, w),
                   reads=[self.r_weff], writes=[self.r_weff])
            c.step("vector", lambda e, w=w, hi=hi, lo=lo: e.tensor_sub(lo, w, hi),
                   reads=[self.r_weff], writes=[self.r_weff])

    def _merge_res(self, dst, *srcs):
        for s in srcs:
            if s.writer is not None:
                dst.readers.append(s.writer)
            dst.readers.extend(s.readers)

    def _fork_res(self, src, *dsts):
        for d in dsts:
            d.writer = src.writer
            d.readers = list(src.readers)

    def emit_crow(self, col0, cap):
        c = self.c
        c.step("gpsimd", lambda e: e.iota(out=self.crow[:, col0:col0 + 8],
                                          pattern=[[cap, 8]], base=0, channel_multiplier=0,
                                          allow_small_or_imprecise_dtypes=True),
               writes=[self.r_crow])

    def emit_moe_prefill(self, X_d, cap, Y_d):
        c = self.c
        z = self.pbf[2]
        c.step("vector", lambda e: e.memset(z[:, 0:4096], 0.0), writes=[self.r_pbf[2]])
        rows = 8 * cap + 1
        for r0 in range(0, rows, 512):
            n = min(512, rows - r0)
            c.step("gpsimd", lambda e, r0=r0, n=n:
                   e.dma_start(out=X_d[0][r0:r0 + n, :], in_=z[:, 0:n * 8]),
                   reads=[self.r_pbf[2]], writes=[X_d[1]], dma=True)
        zf = self.pf[0]
        c.step("vector", lambda e: e.memset(zf[:, :], 0.0), writes=[self.r_pf[0]])
        c.step("gpsimd", lambda e: e.dma_start(out=Y_d[0][8 * cap:8 * cap + 1, :], in_=zf[0:1, :]),
               reads=[self.r_pf[0]], writes=[Y_d[1]], dma=True)

    def moe_route_scatter(self, t128, slot, mskF, pos0, cap, crow_c0, X_d, wsum_src=None):
        c = self.c
        icol = 32 * slot + t128
        q = self.pf[2][:, 104:112]
        t2 = self.pf[2][:, 112:120]
        t3 = self.pf[2][:, 121:129]
        idxf = self.pf[2][:, 120:121]
        crow = self.crow[:, crow_c0:crow_c0 + 8]
        PARK = float(8 * cap)
        c.step("vector", lambda e, mskF=mskF, pos0=pos0, q=q:
               e.tensor_add(q, pos0, crow),
               reads=[self.r_pf[2], self.r_crow], writes=[self.r_pf[2]])
        c.step("vector", lambda e, pos0=pos0, t2=t2:
               e.tensor_scalar(t2, pos0, float(cap), None, ALU.is_ge),
               reads=[self.r_pf[2]], writes=[self.r_pf[2]])
        c.step("vector", lambda e, q=q, t3=t3:
               e.tensor_scalar(t3, q, PARK, None, ALU.subtract),
               reads=[self.r_pf[2]], writes=[self.r_pf[2]])
        c.step("vector", lambda e, t2=t2, t3=t3:
               e.tensor_mul(t3, t3, t2),
               reads=[self.r_pf[2]], writes=[self.r_pf[2]])
        c.step("vector", lambda e, q=q, t3=t3:
               e.tensor_sub(q, q, t3),
               reads=[self.r_pf[2]], writes=[self.r_pf[2]])
        c.step("vector", lambda e, q=q, mskF=mskF:
               e.tensor_mul(q, q, mskF),
               reads=[self.r_pf[2]], writes=[self.r_pf[2]])
        c.step("vector", lambda e, q=q, idxf=idxf:
               e.reduce_sum(idxf, q, axis=AX.X),
               reads=[self.r_pf[2]], writes=[self.r_pf[2]])
        c.step("vector", lambda e, icol=icol, idxf=idxf:
               e.tensor_copy(self.idx_all[:, icol:icol + 1], idxf),
               reads=[self.r_pf[2]], writes=[self.r_idx])
        if wsum_src is not None:
            wap, wres = wsum_src
            c.step("vector", lambda e, icol=icol, wap=wap:
                   e.reduce_sum(self.wcol2[:, icol:icol + 1], wap, axis=AX.X),
                   reads=[wres], writes=[self.r_wcol2])
        if slot == 0:
            sl = t128 % 8
            buf, rbuf = self.pbf[sl % 2], self.r_pbf[sl % 2]
            tile = buf[:, 1024 * (sl // 2):1024 * (sl // 2) + 1024]
            c.step("sync", lambda e, t128=t128, tile=tile:
                   e.dma_start(out=tile, in_=self.xn_d[0][128 * t128:128 * (t128 + 1), :]),
                   reads=[self.xn_d[1]], writes=[rbuf], dma=True)
            self._moe_bounce = (tile, rbuf)
        tile, rbuf = self._moe_bounce
        c.step("gpsimd", lambda e, icol=icol, tile=tile:
               e.indirect_dma_start(
                   out=X_d[0][:, :],
                   out_offset=IndirectOffsetOnAxis(self.idx_all[:, icol:icol + 1], 0),
                   in_=tile, in_offset=None),
               reads=[rbuf, self.r_idx], writes=[X_d[1]], dma=True)

    def _xet_ap(self, buf, d, cap):
        if buf == 0:
            k0 = 4096 // cap
            if d < k0:
                return self.pbf[0][:, cap * d:cap * (d + 1)], self.r_pbf[0]
            return self.pbf[1][:, cap * (d - k0):cap * (d - k0 + 1)], self.r_pbf[1]
        base = 32 * cap
        return self.vres[:, base + cap * d:base + cap * (d + 1)], self.r_xetB

    def emit_moe_experts(self, cap, act, w1scr, w1res, w2scr, w2res, b1W, b2W, X_d, Y_d):
        c = self.c
        half = cap // 2
        tsubs = cap // 128
        self._fork_res(self.r_wslab8, self.r_w8h[0], self.r_w8h[1])
        self._fork_res(self.r_vres, self.r_xetB)
        for e_i in range(8):
            pp = e_i % 2
            for d in range(ND):
                dst, rdst = self._xet_ap(pp, d, cap)
                c.step("scalar", lambda e, e_i=e_i, d=d, dst=dst:
                       e.dma_start_transpose(dst, X_d[0][cap * e_i:cap * (e_i + 1), 128 * d:128 * (d + 1)]),
                       reads=[X_d[1]], writes=[rdst], dma=True)
            self.load_vec_cols(b1W[e_i], 4096, 224)
            c.step("sync", lambda e, e_i=e_i: e.dma_start(out=self.row1k[0:1, :],
                                                          in_=b2W[e_i].rearrange("(a b) -> a b", a=1)),
                   writes=[self.r_row1k], dma=True)
            for j2 in range(2):
                c.step("tensor", lambda e, j2=j2:
                       e.matmul(self.psv(4 + j2), self.onesrow_f[0:1, :],
                                self.row1k[0:1, 512 * j2:512 * (j2 + 1)], start=True, stop=True),
                       reads=[self.r_row1k, self.r_const], writes=[self.r_ps[4 + j2]])
                c.step("vector", lambda e, j2=j2:
                       e.tensor_copy(self.pf[3][:, 512 * j2:512 * (j2 + 1)], self.psv(4 + j2)),
                       reads=[self.r_ps[4 + j2]], writes=[self.r_pf[3]])
            for jg in range(8):
                wb = 4096 * (jg % 2)
                rw = self.r_w8h[jg % 2]
                for dd in range(ND):
                    c.step("sync", lambda e, e_i=e_i, jg=jg, dd=dd, wb=wb:
                           e.dma_start(out=self.wslab8[:, wb + 512 * dd:wb + 512 * (dd + 1)],
                                       in_=w1scr[e_i][128 * dd:128 * (dd + 1), 512 * jg:512 * (jg + 1)]),
                           reads=[w1res], writes=[rw], dma=True)
                for jt in range(4):
                    jc = 4 * jg + jt
                    for th in range(2):
                        bank = 6 + (2 * jt + th) % 2
                        for d in range(ND):
                            xc, rxc = self._xet_ap(pp, d, cap)
                            c.step("tensor", lambda e, d=d, jt=jt, th=th, wb=wb, bank=bank, xc=xc:
                                   e.matmul(self.psv(bank)[:, 0:half],
                                            self.wslab8[:, wb + 512 * d + 128 * jt:wb + 512 * d + 128 * (jt + 1)],
                                            xc[:, half * th:half * (th + 1)],
                                            start=(d == 0), stop=(d == ND - 1)),
                                   reads=[rw, rxc], writes=[self.r_ps[bank]])
                        c.step("scalar", lambda e, jc=jc, th=th, bank=bank:
                               e.activation(out=self.vres[:, cap * jc + half * th:cap * jc + half * (th + 1)],
                                            in_=self.psv(bank)[:, 0:half], func=act,
                                            bias=self.fstat[:, 224 + jc:225 + jc]),
                               reads=[self.r_ps[bank], self.r_fstat], writes=[self.r_vres])
            for op in (0, 512):
                for hg in range(4):
                    wb = 4096 * (hg % 2)
                    rw = self.r_w8h[hg % 2]
                    for i in range(8):
                        hc = 8 * hg + i
                        c.step("sync", lambda e, e_i=e_i, hc=hc, i=i, op=op, wb=wb:
                               e.dma_start(out=self.wslab8[:, wb + 512 * i:wb + 512 * (i + 1)],
                                           in_=w2scr[e_i][128 * hc:128 * (hc + 1), op:op + 512]),
                               reads=[w2res], writes=[rw], dma=True)
                    for i in range(8):
                        hc = 8 * hg + i
                        for ts in range(tsubs):
                            c.step("tensor", lambda e, hc=hc, i=i, ts=ts, wb=wb:
                                   e.matmul(self.psv(ts), self.vres[:, cap * hc + 128 * ts:cap * hc + 128 * (ts + 1)],
                                            self.wslab8[:, wb + 512 * i:wb + 512 * (i + 1)],
                                            start=(hc == 0), stop=(hc == 31)),
                                   reads=[rw, self.r_vres], writes=[self.r_ps[ts]])
                for ts in range(tsubs):
                    yo = self.pf[2][:, 512 * (ts % 2):512 * (ts % 2 + 1)]
                    c.step("vector", lambda e, ts=ts, op=op, yo=yo:
                           e.scalar_tensor_tensor(yo, self.psv(ts), 1.0,
                                                  self.pf[3][:, op:op + 512], ALU.mult, ALU.add),
                           reads=[self.r_ps[ts], self.r_pf[3]], writes=[self.r_pf[2]])
                    c.step("sync", lambda e, e_i=e_i, ts=ts, op=op, yo=yo:
                           e.dma_start(out=Y_d[0][cap * e_i + 128 * ts:cap * e_i + 128 * (ts + 1), op:op + 512],
                                       in_=yo),
                           reads=[self.r_pf[2]], writes=[Y_d[1]], dma=True)
        self._merge_res(self.r_wslab8, self.r_w8h[0], self.r_w8h[1])
        self._merge_res(self.r_vres, self.r_xetB)

    def emit_moe_combine(self, cap, Y_d, nslots, bias_fstat_c0=None):
        c = self.c
        for t128 in range(NT):
            yg = [None, None]
            for s in range(nslots):
                buf = self.pf[s]
                rbuf = self.r_pf[s]
                yg[s] = (buf, rbuf)
                c.step("gpsimd", lambda e, t128=t128, s=s, buf=buf:
                       e.indirect_dma_start(
                           out=buf[:, :], out_offset=None,
                           in_=Y_d[0][:, :],
                           in_offset=IndirectOffsetOnAxis(self.idx_all[:, 32 * s + t128:32 * s + t128 + 1], 0)),
                       reads=[Y_d[1], self.r_idx], writes=[rbuf], dma=True)
            b0, r0 = yg[0]
            w0 = self.wcol2[:, t128:t128 + 1]
            c.step("vector", lambda e, b0=b0, w0=w0:
                   e.tensor_scalar(b0[:, :], b0[:, :], w0, None, ALU.mult),
                   reads=[self.r_wcol2, r0], writes=[r0])
            if nslots == 2:
                b1, r1 = yg[1]
                w1 = self.wcol2[:, 32 + t128:32 + t128 + 1]
                c.step("vector", lambda e, b0=b0, b1=b1, w1=w1:
                       e.scalar_tensor_tensor(b0[:, :], b1[:, :], w1, b0[:, :], ALU.mult, ALU.add),
                       reads=[self.r_wcol2, r0, r1], writes=[r0])
            if bias_fstat_c0 is not None:
                sc = self.fstat[:, bias_fstat_c0 + t128:bias_fstat_c0 + t128 + 1]
                c.step("vector", lambda e, b0=b0, sc=sc:
                       e.scalar_tensor_tensor(b0[:, :], self.pf[3][:, :], sc, b0[:, :], ALU.mult, ALU.add),
                       reads=[self.r_pf[3], self.r_fstat, r0], writes=[r0])
            c.step("sync", lambda e, t128=t128, b0=b0:
                   e.dma_start(out=self.fout_d[0][128 * t128:128 * (t128 + 1), :], in_=b0[:, :]),
                   reads=[r0], writes=[self.fout_d[1]], dma=True)

    def emit_l3_router(self):
        c = self.c
        wg = self.pf[3]
        c.step("sync", lambda e: e.dma_start(out=wg[0:8, 0:16], in_=self.W["L3_ffn_wg"]),
               writes=[self.r_pf[3]], dma=True)
        nrm = self.scol[:, 96:97]
        rb = self.r_scol[12]
        c.step("vector", lambda e: e.scalar_tensor_tensor(self.pf[2][0:8, 0:16], wg[0:8, 0:16], 1.0,
                                                          wg[0:8, 0:16], ALU.mult, ALU.mult),
               reads=[self.r_pf[3]], writes=[self.r_pf[2]])
        c.step("vector", lambda e: e.reduce_sum(nrm[0:8, :], self.pf[2][0:8, 0:16], axis=AX.X),
               reads=[self.r_pf[2]], writes=[rb])
        c.step("scalar", lambda e: e.activation(out=nrm[0:8, :], in_=nrm[0:8, :], func=AF.Sqrt),
               reads=[rb], writes=[rb])
        c.step("vector", lambda e: e.tensor_scalar_max(nrm[0:8, :], nrm[0:8, :], 1e-4),
               reads=[rb], writes=[rb])
        c.step("vector", lambda e: e.reciprocal(nrm[0:8, :], nrm[0:8, :]), reads=[rb], writes=[rb])
        c.step("vector", lambda e: e.tensor_scalar(wg[0:8, 0:16], wg[0:8, 0:16], nrm[0:8, :], None, ALU.mult),
               reads=[rb, self.r_pf[3]], writes=[self.r_pf[3]])
        pv = self.psv(6)[:, 0:128]
        c.step("tensor", lambda e: e.transpose(pv, wg[0:128, 0:128], self.ident_f[:, :]),
               reads=[self.r_pf[3], self.r_const], writes=[self.r_ps[6]])
        wgu = self.pf[2]
        c.step("vector", lambda e: e.tensor_copy(wgu[0:16, 0:8], pv[0:16, 0:8]),
               reads=[self.r_ps[6]], writes=[self.r_pf[2]])
        wgr = self.pf[3]
        c.step("sync", lambda e: e.dma_start(out=wgr[0:16, 0:1024], in_=self.W["L3_ffn_wg_red"]),
               writes=[self.r_pf[3]], dma=True)
        for d in range(ND):
            c.step("tensor", lambda e, d=d:
                   e.matmul(self.psv(7)[:, 0:8], wgr[0:16, 128 * d:128 * (d + 1)], wgu[0:16, 0:8],
                            start=True, stop=True),
                   reads=[self.r_pf[3], self.r_pf[2]], writes=[self.r_ps[7]])
            c.step("vector", lambda e, d=d: e.tensor_copy(self.weff[:, 8 * d:8 * d + 8], self.psv(7)[:, 0:8]),
                   reads=[self.r_ps[7]], writes=[self.r_weff])
        self._weff_hilo(8)

        mx = self.scol[:, 98:99]
        nmx = self.scol[:, 97:98]
        den = self.scol[:, 99:100]
        dump = self.pf[3]
        l3carry = self.scol[:, 90:91]
        rcarry = self.r_scol[11]
        c.step("vector", lambda e: e.memset(l3carry[0:8, :], 0.0), writes=[rcarry])

        def sink(t128, pv, rds):
            lg = self.pf[2][:, 0:8]
            zz8 = self.pf[2][:, 8:16]
            msk = self.pf[2][:, 16:24]
            cum = self.pf[2][:, 24:32]
            mskF = self.pf[2][:, 32:40]
            pos0 = self.pf[2][:, 40:48]
            c.step("vector", lambda e, pv=pv, lg=lg: e.tensor_copy(lg, pv),
                   reads=rds, writes=[self.r_pf[2]])
            c.step("vector", lambda e, lg=lg: e.reduce_max(mx, lg, axis=AX.X),
                   reads=[self.r_pf[2]], writes=[rb])
            c.step("vector", lambda e, lg=lg, msk=msk: e.tensor_scalar(msk, lg, mx, None, ALU.is_equal),
                   reads=[self.r_pf[2], rb], writes=[self.r_pf[2]])
            c.step("vector", lambda e, zz8=zz8: e.memset(zz8, 0.0), writes=[self.r_pf[2]])
            c.step("vector", lambda e, msk=msk, cum=cum, zz8=zz8:
                   e.tensor_tensor_scan(cum, msk, zz8, 0.0, ALU.add, ALU.add),
                   reads=[self.r_pf[2]], writes=[self.r_pf[2]])
            c.step("vector", lambda e, cum=cum: e.tensor_scalar(cum, cum, 1.0, None, ALU.is_equal),
                   reads=[self.r_pf[2]], writes=[self.r_pf[2]])
            c.step("vector", lambda e, msk=msk, cum=cum, mskF=mskF: e.tensor_mul(mskF, msk, cum),
                   reads=[self.r_pf[2]], writes=[self.r_pf[2]])
            c.step("vector", lambda e: e.tensor_scalar_mul(nmx, mx, -1.0),
                   reads=[rb], writes=[rb])
            c.step("scalar", lambda e, lg=lg: e.activation(out=dump[:, 0:8], in_=lg, func=AF.Exp,
                                                           bias=nmx, accum_out=den),
                   reads=[self.r_pf[2], rb], writes=[self.r_pf[3], rb])
            c.step("vector", lambda e: e.reciprocal(den, den), reads=[rb], writes=[rb])
            rc8 = self.rcol[:, 8 * t128:8 * (t128 + 1)]
            c.step("vector", lambda e, rc8=rc8, mskF=mskF:
                   e.tensor_scalar(rc8, mskF, den, None, ALU.mult),
                   reads=[self.r_pf[2], rb], writes=[self.r_rcol])
            bank = t128 % 8
            pvt = self.psv(bank)[:, 0:128]
            c.step("tensor", lambda e, pvt=pvt:
                   e.transpose(pvt, self.pf[2][0:128, 32:160], self.ident_f[:, :]),
                   reads=[self.r_pf[2], self.r_const], writes=[self.r_ps[bank]])
            sc8 = self.pf[3][0:8, 0:128]
            cum8 = self.pf[3][0:8, 128:256]
            zz = self.pf[3][0:8, 256:384]
            c.step("vector", lambda e, pvt=pvt, sc8=sc8: e.tensor_copy(sc8, pvt[0:8, 0:128]),
                   reads=[self.r_ps[bank]], writes=[self.r_pf[3]])
            c.step("vector", lambda e, zz=zz: e.memset(zz, 0.0), writes=[self.r_pf[3]])
            c.step("vector", lambda e, sc8=sc8, cum8=cum8, zz=zz:
                   e.tensor_tensor_scan(cum8, sc8, zz, l3carry[0:8, :], ALU.add, ALU.add),
                   reads=[self.r_pf[3], rcarry], writes=[self.r_pf[3]])
            c.step("vector", lambda e, cum8=cum8: e.tensor_copy(l3carry[0:8, :], cum8[:, 127:128]),
                   reads=[self.r_pf[3]], writes=[rcarry])
            pvt2 = self.psv(bank)[:, 128:256]
            c.step("tensor", lambda e, pvt2=pvt2:
                   e.transpose(pvt2, self.pf[3][0:128, 128:256], self.ident_f[:, :]),
                   reads=[self.r_pf[3], self.r_const], writes=[self.r_ps[bank]])
            c.step("vector", lambda e, pvt2=pvt2, mskF=mskF, pos0=pos0:
                   e.tensor_sub(pos0, pvt2[:, 0:8], mskF),
                   reads=[self.r_ps[bank], self.r_pf[2]], writes=[self.r_pf[2]])
            self.moe_route_scatter(t128, 0, mskF, pos0, 768, 0, self.X3_d,
                                   wsum_src=(rc8, self.r_rcol))
        self.emit_router(8, sink)

    def emit_l3_ffn(self):
        c = self.c
        CAP3 = 768
        self.X3_d = self._dram("X3_d", (8 * CAP3 + 1, D), BF16)
        self.Y3_d = self._dram("Y3_d", (8 * CAP3 + 1, D), F32)
        self.emit_crow(0, CAP3)
        self.emit_moe_prefill(self.X3_d, CAP3, self.Y3_d)
        self.emit_l3_router()
        w1scr, w1res = self.wscr["L3_ffn_w1"]
        w2scr, w2res = self.wscr["L3_ffn_w2"]
        self.emit_moe_experts(CAP3, AF.Gelu, w1scr, w1res, w2scr, w2res,
                              self.W["L3_ffn_b1"], self.W["L3_ffn_b2"],
                              self.X3_d, self.Y3_d)
        self.emit_moe_combine(CAP3, self.Y3_d, 1)

    def emit_layer3(self):
        self.emit_l3_ffn()
        self.emit_cos_sin(3)
        self.emit_qk_proj("L3_attn_Wq", self.qT_d[0], self.qT_d[1], 3, bias_name="L3_attn_bq", bias_col0=128)
        self.emit_qk_proj("L3_attn_Wk", self.kT_d[0], self.kT_d[1], 3, bias_name="L3_attn_bk", bias_col0=136)
        self.emit_v_proj("L3_attn_Wv", bias_name="L3_attn_bv")
        self.emit_attn_core(True, self.qT_d[0], self.qT_d[1], self.kT_d[0], self.kT_d[1])
        self.emit_out_proj("L3_attn_Wo", self.aout_d[0], self.aout_d[1], bias_name="L3_attn_bo")


    def emit_plain_projT(self, wsrc, wres_, dst_d, dst_res, scale=None, wmode="cast"):
        c = self.c
        self.load_wslab8_resident(wsrc, wres_, 1024, mode=wmode)
        for il in range(8):
            i128 = il
            for d in range(ND):
                for tb in range(8):
                    c.step("tensor", lambda e, d=d, il=il, tb=tb:
                           e.matmul(self.psv(tb), self.wslab8[:, 1024 * d + 128 * il:1024 * d + 128 * (il + 1)],
                                    self.big8v(d)[:, 512 * tb:512 * (tb + 1)],
                                    start=(d == 0), stop=(d == ND - 1)),
                           reads=[self.r_wslab8, self.r_big8], writes=[self.r_ps[tb]])
            qt = self.pbf[2]
            rqt = self.r_pbf[2]
            for tb in range(8):
                if scale is None:
                    c.step("vector", lambda e, tb=tb, qt=qt:
                           e.tensor_copy(qt[:, 512 * tb:512 * (tb + 1)], self.psv(tb)),
                           reads=[self.r_ps[tb]], writes=[rqt])
                else:
                    c.step("vector", lambda e, tb=tb, qt=qt:
                           e.tensor_scalar_mul(qt[:, 512 * tb:512 * (tb + 1)], self.psv(tb), scale),
                           reads=[self.r_ps[tb]], writes=[rqt])
                c.step("sync", lambda e, i128=i128, tb=tb, qt=qt:
                       e.dma_start(out=dst_d[128 * i128:128 * (i128 + 1), 512 * tb:512 * (tb + 1)],
                                   in_=qt[:, 512 * tb:512 * (tb + 1)]),
                       reads=[rqt], writes=[dst_res], dma=True)

    def emit_l1_ffn(self):
        c = self.c
        CAP1 = 1280
        self.X1_d = self._dram("X1_d", (8 * CAP1 + 1, D), BF16)
        self.Y1_d = self._dram("Y1_d", (8 * CAP1 + 1, D), F32)
        self.emit_crow(16, CAP1)
        self.emit_moe_prefill(self.X1_d, CAP1, self.Y1_d)
        wgt = self.pf[3]
        c.step("sync", lambda e: e.dma_start(out=wgt[0:8, 0:1024], in_=self.W["L1_ffn_Wg"]),
               writes=[self.r_pf[3]], dma=True)
        for dd in range(ND):
            pvt = self.psv(6)[:, 0:128]
            c.step("tensor", lambda e, dd=dd, pvt=pvt: e.transpose(pvt, wgt[0:128, 128 * dd:128 * (dd + 1)],
                                                                   self.ident_f[:, :]),
                   reads=[self.r_pf[3], self.r_const], writes=[self.r_ps[6]])
            c.step("vector", lambda e, dd=dd, pvt=pvt: e.tensor_copy(self.weff[:, 8 * dd:8 * dd + 8], pvt[:, 0:8]),
                   reads=[self.r_ps[6]], writes=[self.r_weff])
        self._weff_hilo(8)
        mx = self.scol[:, 98:99]
        nmx = self.scol[:, 97:98]
        m2 = self.scol[:, 96:97]
        den = self.scol[:, 99:100]
        v1 = self.scol[:, 95:96]
        v2 = self.scol[:, 94:95]
        sumc = self.scol[:, 93:94]
        rb = self.r_scol[12]
        dump = self.pf[3]
        l1carry = self.scol[:, 90:91]
        rcarry = self.r_scol[11]
        c.step("vector", lambda e: e.memset(l1carry[0:8, :], 0.0), writes=[rcarry])

        def sink(t128, pv, rds):
            lg = self.pf[2][:, 0:8]
            lg2 = self.pf[2][:, 8:16]
            msk2 = self.pf[2][:, 16:24]
            msk1 = self.pf[2][:, 24:32]
            c.step("vector", lambda e, pv=pv, lg=lg: e.tensor_copy(lg, pv),
                   reads=rds, writes=[self.r_pf[2]])
            c.step("vector", lambda e, lg=lg: e.reduce_max(mx, lg, axis=AX.X),
                   reads=[self.r_pf[2]], writes=[rb])
            c.step("vector", lambda e, lg=lg, msk1=msk1: e.tensor_scalar(msk1, lg, mx, None, ALU.is_equal),
                   reads=[self.r_pf[2], rb], writes=[self.r_pf[2]])
            c.step("vector", lambda e, lg=lg, lg2=lg2, msk1=msk1:
                   e.scalar_tensor_tensor(lg2, msk1, -1e30, lg, ALU.mult, ALU.add),
                   reads=[self.r_pf[2]], writes=[self.r_pf[2]])
            c.step("vector", lambda e, lg2=lg2: e.reduce_max(m2, lg2, axis=AX.X),
                   reads=[self.r_pf[2]], writes=[rb])
            c.step("vector", lambda e, lg2=lg2, msk2=msk2: e.tensor_scalar(msk2, lg2, m2, None, ALU.is_equal),
                   reads=[self.r_pf[2], rb], writes=[self.r_pf[2]])
            c.step("vector", lambda e: e.tensor_scalar_mul(nmx, mx, -1.0),
                   reads=[rb], writes=[rb])
            c.step("scalar", lambda e, lg=lg: e.activation(out=dump[:, 0:8], in_=lg, func=AF.Exp,
                                                           bias=nmx, accum_out=den),
                   reads=[self.r_pf[2], rb], writes=[self.r_pf[3], rb])
            c.step("vector", lambda e: e.reciprocal(v1, den), reads=[rb], writes=[rb])
            c.step("vector", lambda e: e.tensor_sub(m2, m2, mx), reads=[rb], writes=[rb])
            c.step("scalar", lambda e: e.activation(out=m2, in_=m2, func=AF.Exp),
                   reads=[rb], writes=[rb])
            c.step("vector", lambda e: e.tensor_mul(v2, m2, v1), reads=[rb], writes=[rb])
            c.step("vector", lambda e: e.tensor_scalar(sumc, v1, v2, 1e-9, ALU.add, ALU.add),
                   reads=[rb], writes=[rb])
            c.step("vector", lambda e: e.reciprocal(sumc, sumc), reads=[rb], writes=[rb])
            c.step("vector", lambda e: e.tensor_mul(v1, v1, sumc), reads=[rb], writes=[rb])
            c.step("vector", lambda e: e.tensor_mul(v2, v2, sumc), reads=[rb], writes=[rb])
            c.step("vector", lambda e: e.tensor_add(sumc, v1, v2), reads=[rb], writes=[rb])
            c.step("vector", lambda e, t128=t128:
                   e.tensor_copy(self.wcol2[:, t128:t128 + 1], v1),
                   reads=[rb], writes=[self.r_wcol2])
            c.step("vector", lambda e, t128=t128:
                   e.tensor_copy(self.wcol2[:, 32 + t128:33 + t128], v2),
                   reads=[rb], writes=[self.r_wcol2])
            c.step("vector", lambda e, t128=t128:
                   e.tensor_copy(self.fstat[:, 224 + t128:225 + t128], sumc),
                   reads=[rb], writes=[self.r_fstat])
            m12 = self.pf[2][:, 32:40]
            pos0 = self.pf[2][:, 40:48]
            c.step("vector", lambda e, msk1=msk1, msk2=msk2, m12=m12: e.tensor_add(m12, msk1, msk2),
                   reads=[self.r_pf[2]], writes=[self.r_pf[2]])
            bank = t128 % 8
            pvt = self.psv(bank)[:, 0:128]
            c.step("tensor", lambda e, pvt=pvt:
                   e.transpose(pvt, self.pf[2][0:128, 32:160], self.ident_f[:, :]),
                   reads=[self.r_pf[2], self.r_const], writes=[self.r_ps[bank]])
            sc8 = self.pf[3][0:8, 0:128]
            cum8 = self.pf[3][0:8, 128:256]
            zz = self.pf[3][0:8, 256:384]
            c.step("vector", lambda e, pvt=pvt, sc8=sc8: e.tensor_copy(sc8, pvt[0:8, 0:128]),
                   reads=[self.r_ps[bank]], writes=[self.r_pf[3]])
            c.step("vector", lambda e, zz=zz: e.memset(zz, 0.0), writes=[self.r_pf[3]])
            c.step("vector", lambda e, sc8=sc8, cum8=cum8, zz=zz:
                   e.tensor_tensor_scan(cum8, sc8, zz, l1carry[0:8, :], ALU.add, ALU.add),
                   reads=[self.r_pf[3], rcarry], writes=[self.r_pf[3]])
            c.step("vector", lambda e, cum8=cum8: e.tensor_copy(l1carry[0:8, :], cum8[:, 127:128]),
                   reads=[self.r_pf[3]], writes=[rcarry])
            pvt2 = self.psv(bank)[:, 128:256]
            c.step("tensor", lambda e, pvt2=pvt2:
                   e.transpose(pvt2, self.pf[3][0:128, 128:256], self.ident_f[:, :]),
                   reads=[self.r_pf[3], self.r_const], writes=[self.r_ps[bank]])
            c.step("vector", lambda e, pvt2=pvt2, m12=m12, pos0=pos0:
                   e.tensor_sub(pos0, pvt2[:, 0:8], m12),
                   reads=[self.r_ps[bank], self.r_pf[2]], writes=[self.r_pf[2]])
            self.moe_route_scatter(t128, 0, msk1, pos0, 1280, 16, self.X1_d)
            self.moe_route_scatter(t128, 1, msk2, pos0, 1280, 16, self.X1_d)
        self.emit_router(8, sink)

        w1scr, w1res = self.wscr["L1_ffn_W1"]
        woscr, wores = self.wscr["L1_ffn_Wo"]
        self.emit_l1_experts(CAP1, w1scr, w1res, woscr, wores)
        c.step("sync", lambda e: e.dma_start(out=self.row1k[0:1, :],
                                             in_=self.W["L1_ffn_bo"].rearrange("(a b) -> a b", a=1)),
               writes=[self.r_row1k], dma=True)
        for j2 in range(2):
            c.step("tensor", lambda e, j2=j2: e.matmul(self.psv(4 + j2), self.onesrow_f[0:1, :],
                                                       self.row1k[0:1, 512 * j2:512 * (j2 + 1)],
                                                       start=True, stop=True),
                   reads=[self.r_row1k, self.r_const], writes=[self.r_ps[4 + j2]])
            c.step("vector", lambda e, j2=j2: e.tensor_copy(self.pf[3][:, 512 * j2:512 * (j2 + 1)],
                                                            self.psv(4 + j2)),
                   reads=[self.r_ps[4 + j2]], writes=[self.r_pf[3]])
        self.emit_moe_combine(CAP1, self.Y1_d, 2, bias_fstat_c0=224)

    def emit_l1_experts(self, cap, w1scr, w1res, woscr, wores):
        c = self.c
        tch = [(0, 512), (512, 512), (1024, cap - 1024)]
        tsubs = cap // 128
        self._fork_res(self.r_wslab8, self.r_w8h[0], self.r_w8h[1])
        self._fork_res(self.r_vres, self.r_xetB)
        for e_i in range(8):
            for d in range(ND):
                dst = self.vres[:, 16 * cap + cap * d:16 * cap + cap * (d + 1)]
                c.step("scalar", lambda e, e_i=e_i, d=d, dst=dst:
                       e.dma_start_transpose(dst, self.X1_d[0][cap * e_i:cap * (e_i + 1), 128 * d:128 * (d + 1)]),
                       reads=[self.X1_d[1]], writes=[self.r_xetB], dma=True)
            self.load_vec_cols(self.W["L1_ffn_b1"][e_i], 4096, 192)
            for pg in range(4):
                for hf in range(2):
                    jb = 512 * pg + 2048 * hf
                    for d in range(ND):
                        c.step("scalar", lambda e, e_i=e_i, jb=jb, hf=hf, d=d:
                               e.dma_start_transpose(
                                   self.wslab8[:, 4096 * hf + 512 * d:4096 * hf + 512 * (d + 1)],
                                   w1scr[e_i][jb:jb + 512, 128 * d:128 * (d + 1)]),
                               reads=[w1res], writes=[self.r_w8h[hf]], dma=True)
                for jt in range(4):
                    jp = 4 * pg + jt
                    sgt = self.pbf[2][:, 1280 * (jt % 3):1280 * (jt % 3) + cap]
                    for hf in range(2):
                        bank = 6 + hf
                        bcol = self.fstat[:, 192 + jp + 16 * hf:193 + jp + 16 * hf]
                        for ti, (t0, tw) in enumerate(tch):
                            for d in range(ND):
                                c.step("tensor", lambda e, hf=hf, jt=jt, d=d, t0=t0, tw=tw, bank=bank:
                                       e.matmul(self.psv(bank)[:, t0 % 512:t0 % 512 + tw],
                                                self.wslab8[:, 4096 * hf + 512 * d + 128 * jt:4096 * hf + 512 * d + 128 * (jt + 1)],
                                                self.vres[:, 16 * cap + cap * d + t0:16 * cap + cap * d + t0 + tw],
                                                start=(d == 0), stop=(d == ND - 1)),
                                       reads=[self.r_w8h[hf], self.r_xetB], writes=[self.r_ps[bank]])
                            if hf == 0:
                                c.step("vector", lambda e, jp=jp, t0=t0, tw=tw, bank=bank, bcol=bcol:
                                       e.tensor_scalar(self.vres[:, cap * jp + t0:cap * jp + t0 + tw],
                                                       self.psv(bank)[:, t0 % 512:t0 % 512 + tw],
                                                       bcol, None, ALU.add),
                                       reads=[self.r_ps[bank], self.r_fstat], writes=[self.r_vres])
                            else:
                                c.step("scalar", lambda e, t0=t0, tw=tw, bank=bank, bcol=bcol, sgt=sgt:
                                       e.activation(out=sgt[:, t0:t0 + tw],
                                                    in_=self.psv(bank)[:, t0 % 512:t0 % 512 + tw],
                                                    func=AF.Sigmoid, bias=bcol),
                                       reads=[self.r_ps[bank], self.r_fstat], writes=[self.r_pbf[2]])
                    c.step("vector", lambda e, jp=jp, sgt=sgt:
                           e.tensor_mul(self.vres[:, cap * jp:cap * (jp + 1)],
                                        self.vres[:, cap * jp:cap * (jp + 1)], sgt),
                           reads=[self.r_pbf[2], self.r_vres], writes=[self.r_vres])
            for oh in (0, 512):
                for th2 in range(2):
                    nts = min(5, tsubs - 5 * th2)
                    for hc in range(16):
                        pq = hc % 2
                        slab = self.wst[pq][:, 0:512]
                        c.step("scalar", lambda e, hc=hc, oh=oh, slab=slab:
                               e.dma_start_transpose(slab, woscr[oh:oh + 512, 128 * hc:128 * (hc + 1)]),
                               reads=[wores], writes=[self.r_wst[pq]], dma=True)
                        for ts in range(nts):
                            tg = 5 * th2 + ts
                            c.step("tensor", lambda e, hc=hc, tg=tg, ts=ts, slab=slab, pq=pq:
                                   e.matmul(self.psv(ts), self.vres[:, cap * hc + 128 * tg:cap * hc + 128 * (tg + 1)],
                                            slab, start=(hc == 0), stop=(hc == 15)),
                                   reads=[self.r_wst[pq], self.r_vres], writes=[self.r_ps[ts]])
                    for ts in range(nts):
                        tg = 5 * th2 + ts
                        yo = self.pf[2][:, 512 * (ts % 2):512 * (ts % 2 + 1)]
                        c.step("vector", lambda e, ts=ts, yo=yo:
                               e.tensor_copy(yo, self.psv(ts)),
                               reads=[self.r_ps[ts]], writes=[self.r_pf[2]])
                        c.step("sync", lambda e, e_i=e_i, tg=tg, oh=oh, yo=yo:
                               e.dma_start(out=self.Y1_d[0][cap * e_i + 128 * tg:cap * e_i + 128 * (tg + 1), oh:oh + 512],
                                           in_=yo),
                               reads=[self.r_pf[2]], writes=[self.Y1_d[1]], dma=True)
        self._merge_res(self.r_wslab8, self.r_w8h[0], self.r_w8h[1])
        self._merge_res(self.r_vres, self.r_xetB)

    def emit_l1_attn(self):
        c = self.c
        SCALE = 64.0 ** -0.25
        LNSQM = math.log(math.sqrt(256.0))
        wq = self.W["L1_attn_wq"].rearrange("d h k -> d (h k)")
        wk = self.W["L1_attn_wk"].rearrange("d h k -> d (h k)")
        self.emit_plain_projT(wq, None, self.qT_d[0], self.qT_d[1], scale=SCALE, wmode="cast")
        self.emit_plain_projT(wk, None, self.kT_d[0], self.kT_d[1], scale=SCALE, wmode="cast")
        self.emit_v_proj("L1_attn_wv", wmode="cast")
        for half in range(2):
            c.step("gpsimd", lambda e, half=half:
                   e.dma_start(out=self.pf[3][:, 0:64], in_=self.W["L1_attn_omega"][128 * half:128 * (half + 1), :]),
                   writes=[self.r_pf[3]], dma=True)
            pv = self.psv(6)[:, 0:128]
            c.step("tensor", lambda e, pv=pv: e.transpose(pv, self.pf[3][0:128, 0:128], self.ident_f[:, :]),
                   reads=[self.r_pf[3], self.r_const], writes=[self.r_ps[6]])
            c.step("vector", lambda e, half=half, pv=pv:
                   e.tensor_copy(self.wslab8[0:64, 128 * half:128 * (half + 1)], pv[0:64, 0:128]),
                   reads=[self.r_ps[6]], writes=[self.r_wslab8])
            c.step("vector", lambda e, half=half, pv=pv:
                   e.tensor_copy(self.wslab8[64:128, 128 * half:128 * (half + 1)], pv[0:64, 0:128]),
                   reads=[self.r_ps[6]], writes=[self.r_wslab8])
        kp, rkp = self.pbf[0], self.r_pbf[0]
        qp, rqp = self.pbf[1], self.r_pbf[1]
        PK0, PQ0, PQT0 = 0, 8192, 16384
        rb = self.r_scol[12]
        for hp in range(8):
            c.step("sync", lambda e, hp=hp: e.dma_start(out=qp[:, :], in_=self.qT_d[0][128 * hp:128 * (hp + 1), :]),
                   reads=[self.qT_d[1]], writes=[rqp], dma=True)
            c.step("sync", lambda e, hp=hp: e.dma_start(out=kp[:, :], in_=self.kT_d[0][128 * hp:128 * (hp + 1), :]),
                   reads=[self.kT_d[1]], writes=[rkp], dma=True)
            for hh in range(2):
                h = 2 * hp + hh
                for (tile, rtile, P0) in ((kp, rkp, PK0), (qp, rqp, PQ0)):
                    uh = tile[64 * hh:64 * (hh + 1), :]
                    for half in range(2):
                        sq = self.wst[half][0:64, 0:2048]
                        c.step("scalar", lambda e, sq=sq, uh=uh, half=half:
                               e.activation(out=sq, in_=uh[:, 2048 * half:2048 * (half + 1)], func=AF.Square),
                               reads=[rtile], writes=[self.r_wst[half]])
                        for sub in range(4):
                            i = 2 * half + (sub % 2)
                            c.step("tensor", lambda e, sq=sq, sub=sub, i=i:
                                   e.matmul(self.psv(i)[0:1, :],
                                            self.onescol_bf[0:64, 0:1],
                                            sq[0:64, 512 * sub:512 * (sub + 1)],
                                            start=True, stop=True),
                                   reads=[self.r_wst[half], self.r_const], writes=[self.r_ps[i]])
                            c.step("vector", lambda e, i=i, sub=sub:
                                   e.tensor_copy(self.row1k[0:1, 0:512], self.psv(i)[0:1, :]),
                                   reads=[self.r_ps[i]], writes=[self.r_row1k])
                            c.step("sync", lambda e, half=half, sub=sub:
                                   e.dma_start(out=self.usq_d[0][0:1, 2048 * half + 512 * sub:
                                                                 2048 * half + 512 * (sub + 1)],
                                               in_=self.row1k[0:1, 0:512]),
                                   reads=[self.r_row1k], writes=[self.usq_d[1]], dma=True)
                    self.load_vec_cols(self.usq_d[0].rearrange("a b -> (a b)"), 4096, 128,
                                       vec_res=self.usq_d[1])
                    omTh = self.wslab8[64 * hh:64 * (hh + 1), 0:256]
                    for t128 in range(NT):
                        pvu = self.psv(t128 % 4)
                        c.step("tensor", lambda e, t128=t128, uh=uh, pvu=pvu, omTh=omTh:
                               e.matmul(pvu[:, 0:256], uh[:, 128 * t128:128 * (t128 + 1)], omTh,
                                        start=True, stop=True),
                               reads=[rtile, self.r_wslab8], writes=[self.r_ps[t128 % 4]])
                        stab = self.scol[:, 92:93]
                        bias = self.scol[:, 91:92]
                        c.step("vector", lambda e, pvu=pvu: e.reduce_max(stab, pvu[:, 0:256], axis=AX.X),
                               reads=[self.r_ps[t128 % 4]], writes=[rb])
                        usqc = self.fstat[:, 128 + t128:129 + t128]
                        c.step("vector", lambda e, usqc=usqc:
                               e.tensor_scalar(bias, usqc, 0.5, LNSQM, ALU.mult, ALU.add),
                               reads=[self.r_fstat, rb], writes=[rb])
                        c.step("vector", lambda e: e.tensor_add(bias, bias, stab),
                               reads=[rb], writes=[rb])
                        c.step("vector", lambda e: e.tensor_scalar_mul(bias, bias, -1.0),
                               reads=[rb], writes=[rb])
                        if P0 == PK0:
                            dst = self.big8[:, P0 + 256 * t128:P0 + 256 * (t128 + 1)]
                            rdst = self.r_big8
                        else:
                            dst = self.pbf[2][:, 2048:2304]
                            rdst = self.r_pbf[2]
                        c.step("scalar", lambda e, pvu=pvu, dst=dst, rdst=rdst:
                               e.activation(out=dst, in_=pvu[:, 0:256], func=AF.Exp, bias=bias),
                               reads=[self.r_ps[t128 % 4], rb], writes=[rdst])
                        c.step("vector", lambda e, dst=dst: e.tensor_scalar_add(dst, dst, 1e-6),
                               reads=[rdst], writes=[rdst])
                        if P0 == PQ0:
                            c.step("sync", lambda e, t128=t128, dst=dst:
                                   e.dma_start(out=self.pq_d[0][128 * t128:128 * (t128 + 1), :], in_=dst),
                                   reads=[rdst], writes=[self.pq_d[1]], dma=True)
                for mc in range(2):
                    c.step("scalar", lambda e, mc=mc:
                           e.dma_start_transpose(self.big8[:, PQT0 + 4096 * mc:PQT0 + 4096 * (mc + 1)],
                                                 self.pq_d[0][:, 128 * mc:128 * (mc + 1)]),
                           reads=[self.pq_d[1]], writes=[self.r_big8], dma=True)
                kvz = self.pbf[2]
                rkvz = self.r_pbf[2]
                for mc in range(2):
                    for sc in range(32):
                        vblk = self.vres[:, 1040 * sc + 65 * h:1040 * sc + 65 * (h + 1)]
                        c.step("tensor", lambda e, mc=mc, sc=sc, vblk=vblk:
                               e.matmul(self.psv(4 + mc)[:, 0:65],
                                        self.big8[:, PK0 + 256 * sc + 128 * mc:PK0 + 256 * sc + 128 * (mc + 1)],
                                        vblk, start=(sc == 0), stop=(sc == 31)),
                               reads=[self.r_big8, self.r_vres], writes=[self.r_ps[4 + mc]])
                for mc in range(2):
                    c.step("vector", lambda e, mc=mc:
                           e.tensor_copy(kvz[:, 128 * mc:128 * mc + 65], self.psv(4 + mc)[:, 0:65]),
                           reads=[self.r_ps[4 + mc]], writes=[rkvz])
                for tb in range(8):
                    for mc in range(2):
                        c.step("tensor", lambda e, tb=tb, mc=mc, kvz=kvz:
                               e.matmul(self.psB[0:65, 0:512], kvz[:, 128 * mc:128 * mc + 65],
                                        self.big8[:, PQT0 + 4096 * mc + 512 * tb:PQT0 + 4096 * mc + 512 * (tb + 1)],
                                        start=(mc == 0), stop=(mc == 1)),
                               reads=[rkvz, self.r_big8], writes=[self.r_ps[4]])
                    c.step("vector", lambda e: e.reciprocal(self.row1k[0:1, 0:512], self.psB[64:65, 0:512]),
                           reads=[self.r_ps[4]], writes=[self.r_row1k])
                    c.step("tensor", lambda e: e.matmul(self.psv(0)[0:64, :], self.onesrow_f[0:1, 0:64],
                                                        self.row1k[0:1, 0:512], start=True, stop=True),
                           reads=[self.r_row1k, self.r_const], writes=[self.r_ps[0]])
                    nb = self.pf[2][0:64, 0:512]
                    c.step("vector", lambda e, nb=nb: e.tensor_copy(nb, self.psB[0:64, 0:512]),
                           reads=[self.r_ps[4]], writes=[self.r_pf[2]])
                    ob = self.pf[2][0:64, 512:1024]
                    c.step("vector", lambda e, nb=nb, ob=ob: e.tensor_mul(ob, nb, self.psv(0)[0:64, :]),
                           reads=[self.r_pf[2], self.r_ps[0]], writes=[self.r_pf[2]])
                    c.step("gpsimd", lambda e, tb=tb, h=h, ob=ob:
                           e.dma_start(out=self.qT_d[0][64 * h:64 * (h + 1), 512 * tb:512 * (tb + 1)], in_=ob),
                           reads=[self.r_pf[2]], writes=[self.qT_d[1]], dma=True)
        for c8 in range(ND):
            c.step("sync", lambda e, c8=c8: e.dma_start(out=self.big8v(c8),
                                                        in_=self.qT_d[0][128 * c8:128 * (c8 + 1), :]),
                   reads=[self.qT_d[1]], writes=[self.r_big8], dma=True)
        wl = self.load_wslab8_resident(self.W["L1_attn_wo"], None, 1024, mode="cast")

        def sink(t128, j0, pv, rds):
            ot = self.pf[2][:, 512 * (t128 % 2):512 * (t128 % 2 + 1)]
            c.step("vector", lambda e, pv=pv, ot=ot: e.tensor_copy(ot, pv),
                   reads=rds, writes=[self.r_pf[2]])
            c.step("sync", lambda e, t128=t128, j0=j0, ot=ot:
                   e.dma_start(out=self.aout_d[0][128 * t128:128 * (t128 + 1), j0:j0 + 512], in_=ot),
                   reads=[self.r_pf[2]], writes=[self.aout_d[1]], dma=True)
        self.emit_proj_tok(lambda kc: self.big8v(kc), self.r_big8, 8, wl, 1024, sink, resident=True)

    def load_gb_bcast(self, gname, bname):
        c = self.c
        gt = self.big8[:, 30720:31744]
        bt = self.big8[:, 31744:32768]
        for (name, dst) in ((gname, gt), (bname, bt)):
            c.step("sync", lambda e, name=name: e.dma_start(out=self.row1k[0:1, :],
                                                            in_=self.W[name].rearrange("(a b) -> a b", a=1)),
                   writes=[self.r_row1k], dma=True)
            for j2 in range(2):
                c.step("tensor", lambda e, j2=j2: e.matmul(self.psv(4 + j2), self.onesrow_f[0:1, :],
                                                           self.row1k[0:1, 512 * j2:512 * (j2 + 1)],
                                                           start=True, stop=True),
                       reads=[self.r_row1k, self.r_const], writes=[self.r_ps[4 + j2]])
                c.step("vector", lambda e, j2=j2, dst=dst:
                       e.tensor_copy(dst[:, 512 * j2:512 * (j2 + 1)], self.psv(4 + j2)),
                       reads=[self.r_ps[4 + j2]], writes=[self.r_big8])
        return ((gt, bt), self.r_big8)

    def emit_proj_fT_cast(self, w_ap, n_out, sink, nin=1024):
        c = self.c
        for ig in range(n_out // 1024):
            for d in range(nin // 128):
                c.step("gpsimd", lambda e, d=d, ig=ig:
                       e.dma_start(out=self.wslab8[:, 1024 * d:1024 * (d + 1)],
                                   in_=w_ap[128 * d:128 * (d + 1), 1024 * ig:1024 * (ig + 1)]),
                       writes=[self.r_wslab8], dma=True)
            for il in range(8):
                i128 = ig * 8 + il
                for d in range(nin // 128):
                    for tb in range(8):
                        c.step("tensor", lambda e, d=d, il=il, tb=tb:
                               e.matmul(self.psv(tb), self.wslab8[:, 1024 * d + 128 * il:1024 * d + 128 * (il + 1)],
                                        self.big8v(d)[:, 512 * tb:512 * (tb + 1)],
                                        start=(d == 0), stop=(d == nin // 128 - 1)),
                               reads=[self.r_wslab8, self.r_big8], writes=[self.r_ps[tb]])
                sink(i128)

    def emit_layer1(self):
        c = self.c
        self.emit_l1_ffn()
        self.emit_l1_attn()
        gb = self.load_gb_bcast("L1_attn_ln1_g", "L1_attn_ln1_b")
        self.emit_ln_merge(self.xnf_d[0], [self.aout_d[0]], None, self.xn_d[0],
                           eps=1e-6, src_res=self.xnf_d[1], add_res=[self.aout_d[1]],
                           dstxn_res=self.xn_d[1], gamma_beta=gb,
                           dst_f32=self.x1_d[0], dstf_res=self.x1_d[1])
        self.emit_xnT(self.xn_d[0], self.xn_d[1])
        self.load_vec_cols(self.W["L1_attn_b1"], 4096, 192)

        def sink1(i128):
            bcol = self.fstat[:, 192 + i128:193 + i128]
            ht = self.pbf[2]
            for tb in range(8):
                c.step("scalar", lambda e, tb=tb, ht=ht, bcol=bcol:
                       e.activation(out=ht[:, 512 * tb:512 * (tb + 1)], in_=self.psv(tb),
                                    func=AF.Gelu, bias=bcol),
                       reads=[self.r_ps[tb], self.r_fstat], writes=[self.r_pbf[2]])
                c.step("sync", lambda e, i128=i128, tb=tb, ht=ht:
                       e.dma_start(out=self.hT_d[0][128 * i128:128 * (i128 + 1), 512 * tb:512 * (tb + 1)],
                                   in_=ht[:, 512 * tb:512 * (tb + 1)]),
                       reads=[self.r_pbf[2]], writes=[self.hT_d[1]], dma=True)
        self.emit_proj_fT_cast(self.W["L1_attn_w1"], 4096, sink1)
        c.step("sync", lambda e: e.dma_start(out=self.row1k[0:1, :],
                                             in_=self.W["L1_attn_b2"].rearrange("(a b) -> a b", a=1)),
               writes=[self.r_row1k], dma=True)
        for j2 in range(2):
            c.step("tensor", lambda e, j2=j2: e.matmul(self.psv(4 + j2), self.onesrow_f[0:1, :],
                                                       self.row1k[0:1, 512 * j2:512 * (j2 + 1)],
                                                       start=True, stop=True),
                   reads=[self.r_row1k, self.r_const], writes=[self.r_ps[4 + j2]])
            c.step("vector", lambda e, j2=j2: e.tensor_copy(self.pf[3][:, 512 * j2:512 * (j2 + 1)],
                                                            self.psv(4 + j2)),
                   reads=[self.r_ps[4 + j2]], writes=[self.r_pf[3]])
        for tgp in range(4):
            trow0 = 1024 * tgp
            for hc in range(32):
                c.step("sync", lambda e, hc=hc, trow0=trow0:
                       e.dma_start(out=self.big8h(hc), in_=self.hT_d[0][128 * hc:128 * (hc + 1),
                                                                        trow0:trow0 + 1024]),
                       reads=[self.hT_d[1]], writes=[self.r_big8], dma=True)
            for j0 in (0, 512):
                for tq in range(2):
                    for hc in range(32):
                        pq = hc % 2
                        slab = self.wst[pq][:, 0:512]
                        c.step("gpsimd", lambda e, hc=hc, j0=j0, slab=slab:
                               e.dma_start(out=slab, in_=self.W["L1_attn_w2"][128 * hc:128 * (hc + 1), j0:j0 + 512]),
                               writes=[self.r_wst[pq]], dma=True)
                        for t in range(4):
                            tl = 4 * tq + t
                            bank = 4 * (tq % 2) + t
                            c.step("tensor", lambda e, hc=hc, tl=tl, slab=slab, bank=bank:
                                   e.matmul(self.psv(bank), self.big8h(hc)[:, 128 * tl:128 * (tl + 1)],
                                            slab, start=(hc == 0), stop=(hc == 31)),
                                   reads=[self.r_wst[pq], self.r_big8], writes=[self.r_ps[bank]])
                    for t in range(4):
                        tl = 4 * tq + t
                        t128 = 8 * tgp + tl
                        bank = 4 * (tq % 2) + t
                        pv = self.psv(bank)
                        ot = self.pf[0][:, 512 * (t % 2):512 * (t % 2 + 1)]
                        c.step("vector", lambda e, pv=pv, ot=ot, j0=j0:
                               e.tensor_add(ot, pv, self.pf[3][:, j0:j0 + 512]),
                               reads=[self.r_ps[bank], self.r_pf[3]], writes=[self.r_pf[0]])
                        c.step("sync", lambda e, t128=t128, j0=j0, ot=ot:
                               e.dma_start(out=self.aout_d[0][128 * t128:128 * (t128 + 1), j0:j0 + 512],
                                           in_=ot),
                               reads=[self.r_pf[0]], writes=[self.aout_d[1]], dma=True)
        gb2 = self.load_gb_bcast("L1_attn_ln2_g", "L1_attn_ln2_b")
        self.emit_ln_merge(self.x1_d[0], [self.aout_d[0]], None, None,
                           eps=1e-6, src_res=self.x1_d[1], add_res=[self.aout_d[1]],
                           gamma_beta=gb2, dst_f32=self.aout_d[0], dstf_res=self.aout_d[1])


    def emit_l0_consts(self):
        c = self.c
        g = lambda fn: c.step("gpsimd", fn, writes=[self.r_const])
        g(lambda e: e.memset(self.tril2_f[:, :], 0.0))
        for b in (0, 64):
            g(lambda e, b=b: e.affine_select(
                out=self.tril2_f[b:b + 64, b:b + 64], in_=self.tril2_f[b:b + 64, b:b + 64],
                pattern=[[-1, 64]], compare_op=ALU.is_ge, fill=1.0,
                base=-1, channel_multiplier=1))
        g(lambda e: e.memset(self.masku[:, :], 0.0))
        g(lambda e: e.affine_select(out=self.masku[:, :], in_=self.masku[:, :],
                                    pattern=[[-1, 64]], compare_op=ALU.is_ge,
                                    fill=1.0, base=-1, channel_multiplier=1))
        g(lambda e: e.memset(self.row63[:, :], 0.0))
        g(lambda e: e.affine_select(out=self.row63[:, :], in_=self.row63[:, :],
                                    pattern=[[0, 128]], compare_op=ALU.not_equal,
                                    fill=1.0, base=-63, channel_multiplier=1))
        g(lambda e: e.memset(self.row127[:, :], 0.0))
        g(lambda e: e.affine_select(out=self.row127[:, :], in_=self.row127[:, :],
                                    pattern=[[0, 128]], compare_op=ALU.not_equal,
                                    fill=1.0, base=-127, channel_multiplier=1))

    def emit_l0_ffn(self):
        c = self.c
        CAP = 640.0
        self.X0_d = self._dram("X0_d", (8 * 640 + 1, D), BF16)
        self.Y0_d = self._dram("Y0_d", (8 * 640 + 1, D), F32)
        self.emit_crow(8, 640)
        self.emit_moe_prefill(self.X0_d, 640, self.Y0_d)
        self.emit_weff_from_natural(self.W["L0_ffn_Wr"], None, 8)
        brb = self.scol[:, 80:88]
        rbrb = self.r_scol[10]
        c.step("sync", lambda e: e.dma_start(out=self.row1k[0:1, 0:8],
                                             in_=self.W["L0_ffn_br"].rearrange("(a b) -> a b", a=1)),
               writes=[self.r_row1k], dma=True)
        c.step("tensor", lambda e: e.matmul(self.psv(6)[:, 0:8], self.onesrow_f[0:1, :],
                                            self.row1k[0:1, 0:8], start=True, stop=True),
               reads=[self.r_row1k, self.r_const], writes=[self.r_ps[6]])
        c.step("vector", lambda e: e.tensor_copy(brb, self.psv(6)[:, 0:8]),
               reads=[self.r_ps[6]], writes=[rbrb])
        mx = self.scol[:, 98:99]
        nmx = self.scol[:, 97:98]
        m2 = self.scol[:, 96:97]
        den = self.scol[:, 99:100]
        p1 = self.scol[:, 95:96]
        p2 = self.scol[:, 94:95]
        carry = self.scol[:, 90:91]
        rb = self.r_scol[12]
        rcarry = self.r_scol[11]
        dump = self.pf[3]
        c.step("vector", lambda e: e.memset(carry[0:8, :], 0.0), writes=[rcarry])

        def sink(t128, pv, rds):
            lg = self.pf[2][:, 0:8]
            lg2 = self.pf[2][:, 8:16]
            msk2 = self.pf[2][:, 16:24]
            msk1 = self.pf[2][:, 24:32]
            m12 = self.pf[2][:, 32:40]
            pos0 = self.pf[2][:, 40:48]
            prob = self.pf[2][:, 48:56]
            c.step("vector", lambda e, pv=pv, lg=lg: e.tensor_add(lg, pv, brb),
                   reads=rds + [rbrb], writes=[self.r_pf[2]])
            c.step("vector", lambda e, lg=lg: e.reduce_max(mx, lg, axis=AX.X),
                   reads=[self.r_pf[2]], writes=[rb])
            c.step("vector", lambda e, lg=lg, msk1=msk1: e.tensor_scalar(msk1, lg, mx, None, ALU.is_equal),
                   reads=[self.r_pf[2], rb], writes=[self.r_pf[2]])
            c.step("vector", lambda e, lg=lg, lg2=lg2, msk1=msk1:
                   e.scalar_tensor_tensor(lg2, msk1, -1e30, lg, ALU.mult, ALU.add),
                   reads=[self.r_pf[2]], writes=[self.r_pf[2]])
            c.step("vector", lambda e, lg2=lg2: e.reduce_max(m2, lg2, axis=AX.X),
                   reads=[self.r_pf[2]], writes=[rb])
            c.step("vector", lambda e, lg2=lg2, msk2=msk2: e.tensor_scalar(msk2, lg2, m2, None, ALU.is_equal),
                   reads=[self.r_pf[2], rb], writes=[self.r_pf[2]])
            c.step("vector", lambda e: e.tensor_scalar_mul(nmx, mx, -1.0), reads=[rb], writes=[rb])
            c.step("scalar", lambda e, lg=lg, prob=prob: e.activation(out=prob, in_=lg, func=AF.Exp,
                                                                      bias=nmx, accum_out=den),
                   reads=[self.r_pf[2], rb], writes=[self.r_pf[2], rb])
            c.step("vector", lambda e: e.reciprocal(den, den), reads=[rb], writes=[rb])
            c.step("vector", lambda e: e.tensor_copy(p1, den), reads=[rb], writes=[rb])
            c.step("vector", lambda e: e.tensor_sub(m2, m2, mx), reads=[rb], writes=[rb])
            c.step("scalar", lambda e: e.activation(out=m2, in_=m2, func=AF.Exp), reads=[rb], writes=[rb])
            c.step("vector", lambda e: e.tensor_mul(p2, m2, den), reads=[rb], writes=[rb])
            c.step("vector", lambda e, msk1=msk1, msk2=msk2, m12=m12: e.tensor_add(m12, msk1, msk2),
                   reads=[self.r_pf[2]], writes=[self.r_pf[2]])
            bank = t128 % 8
            pvt = self.psv(bank)[:, 0:128]
            c.step("tensor", lambda e, m12=m12, pvt=pvt:
                   e.transpose(pvt, self.pf[2][0:128, 32:160], self.ident_f[:, :]),
                   reads=[self.r_pf[2], self.r_const], writes=[self.r_ps[bank]])
            sc8 = self.pf[3][0:8, 0:128]
            c.step("vector", lambda e, pvt=pvt, sc8=sc8: e.tensor_copy(sc8, pvt[0:8, 0:128]),
                   reads=[self.r_ps[bank]], writes=[self.r_pf[3]])
            cum8 = self.pf[3][0:8, 128:256]
            zz = self.pf[3][0:8, 256:384]
            c.step("vector", lambda e, zz=zz: e.memset(zz, 0.0), reads=[], writes=[self.r_pf[3]])
            c.step("vector", lambda e, sc8=sc8, cum8=cum8, zz=zz:
                   e.tensor_tensor_scan(cum8, sc8, zz, carry[0:8, :], ALU.add, ALU.add),
                   reads=[self.r_pf[3], rcarry], writes=[self.r_pf[3]])
            c.step("vector", lambda e, cum8=cum8: e.tensor_copy(carry[0:8, :], cum8[:, 127:128]),
                   reads=[self.r_pf[3]], writes=[rcarry])
            pvt2 = self.psv(bank)[:, 128:256]
            c.step("tensor", lambda e, pvt2=pvt2:
                   e.transpose(pvt2, self.pf[3][0:128, 128:256], self.ident_f[:, :]),
                   reads=[self.r_pf[3], self.r_const], writes=[self.r_ps[bank]])
            c.step("vector", lambda e, pvt2=pvt2, m12=m12, pos0=pos0:
                   e.tensor_sub(pos0, pvt2[:, 0:8], m12),
                   reads=[self.r_ps[bank], self.r_pf[2]], writes=[self.r_pf[2]])
            k0 = self.pf[2][:, 56:64]
            k1 = self.pf[2][:, 64:72]
            c.step("vector", lambda e, pos0=pos0, k0=k0: e.tensor_scalar(k0, pos0, CAP, None, ALU.is_lt),
                   reads=[self.r_pf[2]], writes=[self.r_pf[2]])
            c.step("vector", lambda e, pos0=pos0, msk1=msk1, k1=k1:
                   e.tensor_add(k1, pos0, msk1),
                   reads=[self.r_pf[2]], writes=[self.r_pf[2]])
            c.step("vector", lambda e, k1=k1: e.tensor_scalar(k1, k1, CAP, None, ALU.is_lt),
                   reads=[self.r_pf[2]], writes=[self.r_pf[2]])
            c1 = self.pf[2][:, 72:80]
            c.step("vector", lambda e, msk1=msk1, k0=k0, c1=c1: e.tensor_mul(c1, msk1, k0),
                   reads=[self.r_pf[2]], writes=[self.r_pf[2]])
            c.step("vector", lambda e, c1=c1: e.tensor_scalar(c1, c1, p1, None, ALU.mult),
                   reads=[self.r_pf[2], rb], writes=[self.r_pf[2]])
            c2 = self.pf[2][:, 80:88]
            c.step("vector", lambda e, msk2=msk2, k1=k1, c2=c2: e.tensor_mul(c2, msk2, k1),
                   reads=[self.r_pf[2]], writes=[self.r_pf[2]])
            c.step("vector", lambda e, c2=c2: e.tensor_scalar(c2, c2, p2, None, ALU.mult),
                   reads=[self.r_pf[2], rb], writes=[self.r_pf[2]])
            self.moe_route_scatter(t128, 0, msk1, pos0, 640, 8, self.X0_d,
                                   wsum_src=(c1, self.r_pf[2]))
            self.moe_route_scatter(t128, 1, msk2, pos0, 640, 8, self.X0_d,
                                   wsum_src=(c2, self.r_pf[2]))
        self.emit_router(8, sink)

        w1scr, w1res = self.wscr["L0_ffn_W1"]
        w2scr, w2res = self.wscr["L0_ffn_W2"]
        self.emit_moe_experts(640, AF.Relu, w1scr, w1res, w2scr, w2res,
                              self.W["L0_ffn_b1"], self.W["L0_ffn_b2"],
                              self.X0_d, self.Y0_d)
        self.emit_moe_combine(640, self.Y0_d, 2)


    def emit_l0_attn(self):
        c = self.c
        self.emit_l0_consts()
        wqscr, wqres = self.wscr["L0_attn_Wq"]
        wkscr, wkres = self.wscr["L0_attn_Wk"]
        wvscr, wvres = self.wscr["L0_attn_Wv"]
        self.emit_plain_projT(wqscr, wqres, self.qT_d[0], self.qT_d[1], wmode="xbar")
        self.emit_plain_projT(wvscr, wvres, self.kT_d[0], self.kT_d[1], wmode="xbar")
        for (scr, rs, off) in ((wkscr, wkres, 0), (wvscr, wvres, 1024)):
            wl = self.load_wslab8_resident(scr, rs, 1024, mode="xbar")

            def sink(t128, j0, pv, rds, off=off):
                ot = self.pbf[2][:, 512 * (t128 % 2):512 * (t128 % 2 + 1)]
                c.step("vector", lambda e, pv=pv, ot=ot: e.tensor_copy(ot, pv),
                       reads=rds, writes=[self.r_pbf[2]])
                c.step("sync", lambda e, t128=t128, j0=j0, ot=ot, off=off:
                       e.dma_start(out=self.h_d[0][128 * t128:128 * (t128 + 1), off + j0:off + j0 + 512],
                                   in_=ot),
                       reads=[self.r_pbf[2]], writes=[self.h_d[1]], dma=True)
            self.emit_proj_tok(lambda kc: self.big8v(kc), self.r_big8, 8, wl, 1024, sink,
                               resident=True)
        for (name, dstp, half) in (("L0_attn_W_gamma", 2, 0), ("L0_attn_b_gamma", 2, 1),
                                   ("L0_attn_W_lambda", 3, 0), ("L0_attn_b_lambda", 3, 1)):
            for hrow in range(2):
                c.step("sync", lambda e, name=name, hrow=hrow:
                       e.dma_start(out=self.row1k[0:1, 0:512], in_=self.W[name][hrow:hrow + 1, :]),
                       writes=[self.r_row1k], dma=True)
                pvb = self.psv(6)
                c.step("tensor", lambda e, pvb=pvb, hrow=hrow:
                       e.matmul(pvb[0:64, :] if hrow == 0 else pvb[64:128, :],
                                self.onesrow_f[0:1, 0:64], self.row1k[0:1, 0:512],
                                start=True, stop=True),
                       reads=[self.r_row1k, self.r_const], writes=[self.r_ps[6]])
                c.step("vector", lambda e, dstp=dstp, half=half, hrow=hrow, pvb=pvb:
                       e.tensor_copy(self.pf[dstp][64 * hrow:64 * (hrow + 1), 512 * half:512 * (half + 1)],
                                     pvb[0:64, :] if hrow == 0 else pvb[64:128, :]),
                       reads=[self.r_ps[6]], writes=[self.r_pf[dstp]])
        c.step("vector", lambda e: e.memset(self.wslab8[:, 0:4096], 0.0), writes=[self.r_wslab8])
        NCH = 64
        kt, rkt = self.pbf[0], self.r_pbf[0]
        qv, rqv = self.pbf[2], self.r_pbf[2]
        rb = self.r_scol[12]
        for ci in range(NCH):
            t0 = 64 * ci
            if ci % 4 == 0:
                for c8 in range(ND):
                    c.step("sync", lambda e, c8=c8, t0=t0:
                           e.dma_start(out=qv[:, 256 * c8:256 * (c8 + 1)],
                                       in_=self.qT_d[0][128 * c8:128 * (c8 + 1), t0:t0 + 256]),
                           reads=[self.qT_d[1]], writes=[rqv], dma=True)
                    c.step("sync", lambda e, c8=c8, t0=t0:
                           e.dma_start(out=qv[:, 2048 + 256 * c8:2048 + 256 * (c8 + 1)],
                                       in_=self.kT_d[0][128 * c8:128 * (c8 + 1), t0:t0 + 256]),
                           reads=[self.kT_d[1]], writes=[rqv], dma=True)
            co = 64 * (ci % 4)
            for hh in range(2):
                c.step("sync", lambda e, t0=t0, hh=hh:
                       e.dma_start(out=kt[64 * hh:64 * (hh + 1), 0:512],
                                   in_=self.h_d[0][t0:t0 + 64, 512 * hh:512 * (hh + 1)]),
                       reads=[self.h_d[1]], writes=[rkt], dma=True)
                c.step("sync", lambda e, t0=t0, hh=hh:
                       e.dma_start(out=kt[64 * hh:64 * (hh + 1), 512:1024],
                                   in_=self.h_d[0][t0:t0 + 64, 1024 + 512 * hh:1024 + 512 * (hh + 1)]),
                       reads=[self.h_d[1]], writes=[rkt], dma=True)
            ktile = kt[:, 0:512]
            vtile = kt[:, 512:1024]
            expL = kt[:, 1024:1536]
            Ktil = kt[:, 1536:2048]
            kn = self.pf[0][:, 0:512]
            nrm = self.scol[:, 89:90]
            c.step("vector", lambda e, kn=kn, ktile=ktile:
                   e.scalar_tensor_tensor(kn, ktile, 1.0, ktile, ALU.mult, ALU.mult, accum_out=nrm),
                   reads=[rkt], writes=[self.r_pf[0], rb])
            c.step("scalar", lambda e: e.activation(out=nrm, in_=nrm, func=AF.Sqrt),
                   reads=[rb], writes=[rb])
            c.step("vector", lambda e: e.tensor_scalar_max(nrm, nrm, 1e-12), reads=[rb], writes=[rb])
            c.step("vector", lambda e: e.reciprocal(nrm, nrm), reads=[rb], writes=[rb])
            c.step("vector", lambda e, kn=kn, ktile=ktile:
                   e.tensor_scalar(kn, ktile, nrm, None, ALU.mult),
                   reads=[rkt, rb], writes=[self.r_pf[0]])
            gm = self.pf[0][:, 512:1024]
            lm = self.pf[1][:, 0:512]
            lr = self.pf[1][:, 512:1024]
            c.step("vector", lambda e, kn=kn, gm=gm:
                   e.tensor_mul(gm, kn, self.pf[2][:, 0:512]),
                   reads=[self.r_pf[0], self.r_pf[2]], writes=[self.r_pf[0]])
            c.step("vector", lambda e, gm=gm:
                   e.tensor_add(gm, gm, self.pf[2][:, 512:1024]),
                   reads=[self.r_pf[0], self.r_pf[2]], writes=[self.r_pf[0]])
            c.step("scalar", lambda e, gm=gm: e.activation(out=gm, in_=gm, func=AF.Sigmoid),
                   reads=[self.r_pf[0]], writes=[self.r_pf[0]])
            c.step("vector", lambda e, kn=kn, lm=lm:
                   e.tensor_mul(lm, kn, self.pf[3][:, 0:512]),
                   reads=[self.r_pf[0], self.r_pf[3]], writes=[self.r_pf[1]])
            c.step("vector", lambda e, lm=lm:
                   e.tensor_add(lm, lm, self.pf[3][:, 512:1024]),
                   reads=[self.r_pf[1], self.r_pf[3]], writes=[self.r_pf[1]])
            c.step("scalar", lambda e, lm=lm: e.activation(out=lm, in_=lm, func=AF.Sigmoid),
                   reads=[self.r_pf[1]], writes=[self.r_pf[1]])
            c.step("vector", lambda e, lm=lm: e.tensor_scalar(lm, lm, -1.0, 1.0, ALU.mult, ALU.add),
                   reads=[self.r_pf[1]], writes=[self.r_pf[1]])
            c.step("vector", lambda e, gm=gm, lm=lm, lr=lr: e.tensor_mul(lr, gm, lm),
                   reads=[self.r_pf[0], self.r_pf[1]], writes=[self.r_pf[1]])
            c.step("scalar", lambda e, lr=lr, gm=gm: e.activation(out=gm, in_=lr, func=AF.Ln),
                   reads=[self.r_pf[1]], writes=[self.r_pf[0]])
            pL = self.psv(0)
            c.step("tensor", lambda e, gm=gm, pL=pL:
                   e.matmul(pL, self.tril2_f[:, :], gm, start=True, stop=True),
                   reads=[self.r_pf[0], self.r_const], writes=[self.r_ps[0]])
            c.step("scalar", lambda e, pL=pL, expL=expL:
                   e.activation(out=expL, in_=pL, func=AF.Exp),
                   reads=[self.r_ps[0]], writes=[rkt])
            c.step("scalar", lambda e, pL=pL, Ktil=Ktil:
                   e.activation(out=Ktil, in_=pL, func=AF.Exp, scale=-1.0),
                   reads=[self.r_ps[0]], writes=[rkt])
            c.step("vector", lambda e, lr=lr, lr2=lr: e.tensor_scalar(self.pf[1][:, 512:1024],
                                                                      lr, -1.0, 1.0, ALU.mult, ALU.add),
                   reads=[self.r_pf[1]], writes=[self.r_pf[1]])
            c.step("vector", lambda e, kn=kn, Ktil=Ktil:
                   e.tensor_mul(Ktil, Ktil, kn),
                   reads=[rkt, self.r_pf[0]], writes=[rkt])
            c.step("vector", lambda e, Ktil=Ktil:
                   e.tensor_mul(Ktil, Ktil, self.pf[1][:, 512:1024]),
                   reads=[rkt, self.r_pf[1]], writes=[rkt])
            ptm = self.pbf[1][:, 0:64]
            for hh in range(2):
                pp = self.psv(1)[:, 0:64]
                for vc in range(4):
                    vcg = 4 * hh + vc
                    c.step("tensor", lambda e, vcg=vcg, co=co, pp=pp, hh=hh:
                           e.matmul(pp[0:64, :] if hh == 0 else pp[64:128, :],
                                    qv[:, 2048 + 256 * vcg + co:2048 + 256 * vcg + co + 64],
                                    qv[:, 256 * vcg + co:256 * vcg + co + 64],
                                    start=(vcg % 4 == 0), stop=(vcg % 4 == 3)),
                           reads=[rqv], writes=[self.r_ps[1]])
                mslice = ptm[0:64, :] if hh == 0 else ptm[64:128, :]
                pps = pp[0:64, :] if hh == 0 else pp[64:128, :]
                c.step("vector", lambda e, mslice=mslice, pps=pps:
                       e.tensor_mul(mslice, pps, self.masku[:, :]),
                       reads=[self.r_ps[1], self.r_const], writes=[self.r_pbf[1]])
            py = self.psv(2)
            for hh in range(2):
                outs = py[0:64, :] if hh == 0 else py[64:128, :]
                c.step("tensor", lambda e, outs=outs, hh=hh:
                       e.matmul(outs, ptm[64 * hh:64 * (hh + 1), :], Ktil[64 * hh:64 * (hh + 1), :],
                                start=True, stop=False),
                       reads=[self.r_pbf[1], rkt], writes=[self.r_ps[2]])
                for vc in range(4):
                    vcg = 4 * hh + vc
                    c.step("tensor", lambda e, outs=outs, vcg=vcg, co=co, vc=vc, hh=hh:
                           e.matmul(outs, qv[:, 256 * vcg + co:256 * vcg + co + 64],
                                    self.wslab8[:, 2048 * hh + 512 * vc:2048 * hh + 512 * (vc + 1)],
                                    start=False, stop=(vc == 3)),
                           reads=[rqv, self.r_wslab8], writes=[self.r_ps[2]])
            ytile = self.pf[1][:, 0:512]
            c.step("vector", lambda e, ytile=ytile, expL=expL, py=py:
                   e.tensor_mul(ytile, py, expL),
                   reads=[self.r_ps[2], rkt], writes=[self.r_pf[1]])
            for kc4 in range(4):
                pvt = self.psv(3)[:, 0:128]
                c.step("tensor", lambda e, kc4=kc4, ytile=ytile, pvt=pvt:
                       e.transpose(pvt, ytile[:, 128 * kc4:128 * (kc4 + 1)], self.ident_f[:, :]),
                       reads=[self.r_pf[1], self.r_const], writes=[self.r_ps[3]])
                for hh in range(2):
                    dchunk = (512 * hh + 128 * kc4) // 128
                    dst = self.big8v(dchunk)[:, t0:t0 + 64]
                    c.step("vector", lambda e, dst=dst, pvt=pvt, hh=hh:
                           e.tensor_copy(dst, pvt[:, 64 * hh:64 * (hh + 1)]),
                           reads=[self.r_ps[3]], writes=[self.r_big8])
            for hh in range(2):
                pbc = self.psv(4)
                c.step("tensor", lambda e, pbc=pbc, expL=expL, hh=hh:
                       e.matmul(pbc, self.row63[:, :] if hh == 0 else self.row127[:, :],
                                expL, start=True, stop=True),
                       reads=[rkt, self.r_const], writes=[self.r_ps[4]])
                bcs = self.pbf[1][:, 64:576]
                c.step("vector", lambda e, bcs=bcs, pbc=pbc: e.tensor_copy(bcs, pbc),
                       reads=[self.r_ps[4]], writes=[self.r_pbf[1]])
                for vc in range(4):
                    pu = self.psv(5)
                    c.step("tensor", lambda e, pu=pu, hh=hh, vc=vc:
                           e.matmul(pu, vtile[64 * hh:64 * (hh + 1), 128 * vc:128 * (vc + 1)],
                                    Ktil[64 * hh:64 * (hh + 1), :],
                                    start=True, stop=True),
                           reads=[rkt], writes=[self.r_ps[5]])
                    sv = self.wslab8[:, 2048 * hh + 512 * vc:2048 * hh + 512 * (vc + 1)]
                    tband = self.pbf[1][:, 576:1088]
                    c.step("vector", lambda e, sv=sv, pu=pu, tband=tband:
                           e.tensor_add(tband, sv, pu),
                           reads=[self.r_wslab8, self.r_ps[5]], writes=[self.r_pbf[1]])
                    c.step("vector", lambda e, sv=sv, tband=tband, bcs=bcs:
                           e.tensor_mul(sv, tband, bcs),
                           reads=[self.r_pbf[1]], writes=[self.r_wslab8])
        woscr, wores = self.wscr["L0_attn_Wo"]
        wl = self.load_wslab8_resident(woscr, wores, 1024, mode="xbar")

        def sink(t128, j0, pv, rds):
            ot = self.pf[2][:, 512 * (t128 % 2):512 * (t128 % 2 + 1)]
            c.step("vector", lambda e, pv=pv, ot=ot: e.tensor_copy(ot, pv),
                   reads=rds, writes=[self.r_pf[2]])
            c.step("sync", lambda e, t128=t128, j0=j0, ot=ot:
                   e.dma_start(out=self.aout_d[0][128 * t128:128 * (t128 + 1), j0:j0 + 512], in_=ot),
                   reads=[self.r_pf[2]], writes=[self.aout_d[1]], dma=True)
        self.emit_proj_tok(lambda kc: self.big8v(kc), self.r_big8, 8, wl, 1024, sink, resident=True)

    def emit_layer0(self):
        self.emit_l0_ffn()
        self.emit_l0_attn()

    def emit_layer2(self):
        self.emit_cos_sin(2)
        self.emit_qk_proj("L2_attn_Wq", self.qT_d[0], self.qT_d[1], 2)
        self.emit_qk_proj("L2_attn_Wk", self.kT_d[0], self.kT_d[1], 2)
        self.emit_v_proj("L2_attn_Wv")
        self.emit_l2_ffn()
        self.emit_attn_core(False, self.qT_d[0], self.qT_d[1], self.kT_d[0], self.kT_d[1])
        self.emit_out_proj("L2_attn_Wo", self.aout_d[0], self.aout_d[1])


def build_program(layers=(0, 1, 2, 3)):
    p = Prog(layers)
    p.emit_consts()
    p.emit_rope_consts()
    p.emit_prepass()
    if 2 in layers:
        p.emit_quantize(p.W["L2_ffn_W_fc"], 4096, 1024, p.wfcq_d[0], p.wfcq_d[1], "fc")
        p.emit_quantize(p.W["L2_ffn_W_proj"], 1024, 4096, p.wprojq_d[0], p.wprojq_d[1], "pj")
    cur, cur_res = p.x_in, p.r_xin
    adds = []
    for li, l in enumerate(layers):
        lo = l in (0, 1, 3)
        f32 = l == 1
        if adds:
            dst, dst_res = p.xres[li % 2]
            p.emit_ln_merge(cur, [a[0] for a in adds], dst, p.xn_d[0],
                            src_res=cur_res, add_res=[a[1] for a in adds],
                            dstr_res=dst_res, dstxn_res=p.xn_d[1], emit_lo=lo, emit_f32=f32)
            cur, cur_res = dst, dst_res
        else:
            p.emit_ln_merge(cur, [], None, p.xn_d[0], src_res=cur_res,
                            add_res=[], dstr_res=None, dstxn_res=p.xn_d[1], emit_lo=lo, emit_f32=f32)
        p.emit_xnT(p.xn_d[0], p.xn_d[1])
        getattr(p, "emit_layer%d" % l)()
        adds = [p.aout_d, p.fout_d]
    p.emit_final(cur, cur_res, [a[0] for a in adds], [a[1] for a in adds])
    return p.finish()


def kernel(**inputs):
    nc = build_program((0, 1, 2, 3))
    x = inputs["L0_attn_x"]
    in_maps = []
    for cid in range(8):
        m = {"x": np.ascontiguousarray(x[cid])}
        for name in WSPECS:
            m[name] = inputs[name]
        in_maps.append(m)
    res = run_bass_kernel_spmd(nc, in_maps, core_ids=list(range(8)))
    return np.stack([r["out"] for r in res.results]).astype(np.float32)

